# version 11
# speedup vs baseline: 1.4978x; 1.0748x over previous
import numpy as np
import ml_dtypes
import concourse.bass as bass
import concourse.mybir as mybir
from concourse.bass_utils import run_bass_kernel_spmd

F32 = mybir.dt.float32
BF16 = mybir.dt.bfloat16
I32 = mybir.dt.int32
AF = mybir.ActivationFunctionType
ALU = mybir.AluOpType
AX = mybir.AxisListType
BF = ml_dtypes.bfloat16

ENGINES = ("sync", "vector", "scalar", "tensor", "gpsimd")
D = 1024
NCH = 8
NFH = 24
EPS = 1e-5
INV127 = float(np.float32(1.0 / 127.0))


class Sem:
    __slots__ = ("h", "n", "name")

    def __init__(self, handle, name=""):
        self.h = handle
        self.n = 0
        self.name = name


class Prog:
    def __init__(self, nc):
        self.nc = nc
        self.streams = {e: [] for e in ENGINES}
        self.nins = 0

    def sem(self, name):
        return Sem(self.nc.semaphore(name).__enter__(), name)

    def emit(self, engine, fn):
        self.streams[engine].append(fn)

    def wait(self, engine, sem, val=None):
        v = sem.n if val is None else val
        if v <= 0:
            return
        self.emit(engine, lambda eng, s=sem.h, v=v: eng.wait_ge(s, v))

    def op(self, engine, method, *args, inc=None, **kw):
        self.nins += 1
        if inc is not None:
            sem, k = inc if isinstance(inc, tuple) else (inc, 1)
            sem.n += k

            def fn(eng, m=method, a=args, kw=kw, s=sem.h, k=k):
                getattr(eng, m)(*a, **kw).then_inc(s, k)
        else:
            def fn(eng, m=method, a=args, kw=kw):
                getattr(eng, m)(*a, **kw)
        self.emit(engine, fn)

    def dma(self, engine, out, in_, sem):
        self.op(engine, "dma_start", inc=(sem, 16), out=out, in_=in_)

    def build(self, block):
        for e in ENGINES:
            stream = self.streams[e]
            if not stream:
                continue

            def runner(eng, stream=stream):
                for fn in stream:
                    fn(eng)
            getattr(block, e)(runner)


class Banks:

    def __init__(self, p):
        self.p = p
        self.pend = {i: [] for i in range(8)}

    def use(self, i, engine):
        for sem, val in self.pend[i]:
            self.p.wait(engine, sem, val)
        self.pend[i] = []

    def done(self, i, sem):
        self.pend[i].append((sem, sem.n))


class Slots:

    def __init__(self, p):
        self.p = p
        self.pend = {}

    def use(self, key, engine):
        for sem, val in self.pend.get(key, ()):
            self.p.wait(engine, sem, val)
        self.pend[key] = []

    def done(self, key, sem):
        self.pend.setdefault(key, []).append((sem, sem.n))


class Ring:

    def __init__(self, p, area_bf, nslots=8):
        self.p = p
        self.area = area_bf
        self.n = nslots
        self.nmax = nslots
        self.sems = [p.sem(f"wr{i}") for i in range(nslots)]
        self.freed = {i: [] for i in range(nslots)}
        self.next = 0

    def load(self, dram_ap, width_slots=1, shape=None, cols=None):
        if self.next + width_slots > self.n:
            self.next = 0
        i0 = self.next
        self.next += width_slots
        slots = list(range(i0, i0 + width_slots))
        for i in slots:
            for sem, val in self.freed[i]:
                self.p.wait("gpsimd", sem, val)
            self.freed[i] = []
        if cols is None and shape is not None:
            n = 1
            for d in shape:
                n *= d
            cols = n
        ncols = cols if cols is not None else width_slots * 1024
        flat = self.area[:, i0 * 1024:i0 * 1024 + ncols]
        view = flat
        if shape is not None:
            view = flat.rearrange("p (a b) -> p a b", b=shape[-1])
        sem = self.sems[i0]
        self.p.dma("gpsimd", flat[0:64], dram_ap[0:64], sem)
        self.p.dma("gpsimd", flat[64:128], dram_ap[64:128], sem)
        return view, (slots, sem, sem.n)

    def use(self, token, engine="tensor"):
        _, sem, val = token
        self.p.wait(engine, sem, val)

    def free(self, token, sem):
        slots, _, _ = token
        for i in slots:
            self.freed[i].append((sem, sem.n))

    def barrier(self):
        for i in range(self.nmax):
            for sem, val in self.freed[i]:
                self.p.wait("gpsimd", sem, val)
            self.freed[i] = []
        self.next = 0


def build_program(T=2048, n_dram_seqs=2, dbg_stop=None):
    NT = max(1, T // 512)
    TTW = min(512, T)
    NKC = T // 128
    NE = min(16, NKC)
    nc = bass.Bass("TRN2", target_bir_lowering=False, debug=False,
                   detect_race_conditions=False, dynamic_dma_scratch_size=16384)

    def din(name, shape, dtype=F32):
        return nc.dram_tensor(name, list(shape), dtype, kind="ExternalInput").ap()

    x_h = din("x", [n_dram_seqs, NCH, 128, T])
    out_h = nc.dram_tensor("out", [n_dram_seqs, NCH, 128, T], F32,
                           kind="ExternalOutput").ap()
    aq_h = din("aq", [8, 128, 8, 128])
    ak_h = din("ak", [8, 128, 8, 128])
    ao_h = din("ao", [8, 128, 8, 128])
    av_h = din("av", [2, 128, 8, 512])
    aqb_h = din("aqb", [128, 8])
    akb_h = din("akb", [128, 8])
    avb_h = din("avb", [128, 8])
    aob_h = din("aob", [128, 8])
    pw1_h = din("pw1t", [3, 128, 8, 512])
    pb1_h = din("pb1", [128, 12])
    pw2_h = din("pw2t", [128, 192])
    pb2_h = din("pb2t", [16, 3])
    sw1_h = din("sw1t", [128, 8, 512])
    sb1_h = din("sb1", [128, 4])
    sw2_h = din("sw2t", [128, 12])
    sb2_h = din("sb2", [3, 1])
    bq_h = din("bq", [8, 128, 8, 128])
    bk_h = din("bk", [8, 128, 8, 128])
    bv_h = din("bv", [2, 128, 8, 512])
    bqb_h = din("bqb", [128, 8])
    bkb_h = din("bkb", [128, 8])
    bvrow_h = din("bvrow", [1, 1024])
    fgu_h = din("fgu", [NFH, 128, 16, 128], I32)
    fd_h = din("fd", [8, 128, NFH, 128], I32)
    mask_h = din("mask", [128, 128], BF16)
    colsel_h = din("colsel", [128, 8, 8], BF16)
    if dbg_stop:
        dbgA_h = nc.dram_tensor("dbgA", [128, NCH, T], BF16, kind="ExternalOutput").ap()
        dbgB_h = nc.dram_tensor("dbgB", [128, NCH, T], BF16, kind="ExternalOutput").ap()
        dbgC_h = nc.dram_tensor("dbgC", [128, NKC, 1024], BF16, kind="ExternalOutput").ap()
        dbgX_h = nc.dram_tensor("dbgX", [128, NCH, T], BF16, kind="ExternalOutput").ap()
        dbgR_h = nc.dram_tensor("dbgR", [128, NCH, T], F32, kind="ExternalOutput").ap()
        dbgS_h = nc.dram_tensor("dbgS", [128, 192], F32, kind="ExternalOutput").ap()
        dbgN_h = nc.dram_tensor("dbgN", [4, 512], F32, kind="ExternalOutput").ap()
        dbgQ_h = nc.dram_tensor("dbgQ", [128, 2048], BF16, kind="ExternalOutput").ap()

    p = Prog(nc)
    banks = Banks(p)
    esl = Slots(p)

    te = p.sem("te")
    ve = p.sem("ve")
    se = p.sem("se")
    ge = p.sem("ge")
    qx = p.sem("qx")
    qo = p.sem("qo")
    ch = p.sem("ch")
    qLs = [p.sem(f"qL{i}") for i in range(4)]

    sb_keep = []

    def alloc(name, shape, dtype):
        t = nc.sbuf_tensor(name, list(shape), dtype).__enter__()
        sb_keep.append(t)
        return t

    xres_t = alloc("xres", [128, NCH, T], F32)
    xln_t = alloc("xln", [128, NCH, T], BF16)
    A_t = alloc("Abuf", [128, NCH, T], BF16)
    B_t = alloc("Bbuf", [128, NCH, T], BF16)
    C_t = alloc("Cbuf", [128, NKC, 1024], BF16)
    ring_t = alloc("wring", [128, 6 * 1024], BF16)
    mask_t = alloc("mask_s", [128, 128], BF16)
    colsel_t = alloc("colsel_s", [128, 8, 8], BF16)
    ones64_t = alloc("ones64", [128, 64], BF16)
    aqb_t = alloc("aqb_s", [128, 8], F32)
    akb_t = alloc("akb_s", [128, 8], F32)
    avb_t = alloc("avb_s", [128, 8], F32)
    aob_t = alloc("aob_s", [128, 8], F32)
    bopr_t = alloc("bopr", [128, 8], F32)
    avbb_t = alloc("avbb", [128, 8], BF16)
    bqb_t = alloc("bqb_s", [128, 8], F32)
    bkb_t = alloc("bkb_s", [128, 8], F32)
    pb1_t = alloc("pb1_s", [128, 12], F32)
    pb2_t = alloc("pb2_s", [16, 3], F32)
    sb1_t = alloc("sb1_s", [128, 4], F32)
    sb2_t = alloc("sb2_s", [3, 1], F32)
    xm_t = alloc("xm", [128, 8], F32)
    xmb_t = alloc("xmb", [128, 8], BF16)
    hid_t = alloc("hid", [128, 12], F32)
    hidb_t = alloc("hidb", [128, 12], BF16)
    phid_t = alloc("phid", [128, 4], F32)
    phidb_t = alloc("phidb", [128, 4], BF16)
    eg_t = alloc("eg", [16, 3], F32)
    pl3_t = alloc("pl3", [3, 1], F32)
    g48_t = alloc("g48", [1, 48], F32)
    hg_t = alloc("hg", [1, 16], F32)
    srow_t = alloc("srow", [1, 32], F32)
    gsc_t = alloc("gsc", [128, 16], F32)
    gbi_t = gsc_t
    qsc_t = alloc("qsc", [128, 8], F32)
    bqg_t = alloc("bqg", [128, 8], F32)
    nba_t = alloc("nba", [128, 1], F32)
    negf_t = alloc("negf", [128, 384], BF16)
    tiny_t = alloc("tiny", [4, 64], F32)
    nbb_t = alloc("nbb", [128, 1], F32)
    rzb_t = alloc("rzb", [128, 1], F32)
    silu_t = alloc("silu", [128, 512], BF16)
    silu2_t = silu_t
    dvs_t = alloc("dvs", [128, 1536], F32) if T < 2048 else None

    xln_f = xln_t[:].rearrange("p c t -> p (c t)")
    xln_f32 = xln_f.bitcast(F32)
    A_f = A_t[:].rearrange("p c t -> p (c t)")
    B_f = B_t[:].rearrange("p c t -> p (c t)")
    B_f32 = B_f.bitcast(F32)
    C_f = C_t[:].rearrange("p c t -> p (c t)")

    R_bc = B_f[:, 0:T]
    MR_bc = B_f[:, T:2 * T]
    tpass = B_f[:, 2 * T:3 * T]
    rrow_bf = B_f[0:4, 3 * T:3 * T + 512]
    mrrow_bf = B_f[0:4, 3 * T + 512:3 * T + 1024]
    A_f32 = A_f.bitcast(F32)
    lnm = A_f32[0:4, T:T + 512]
    lns2 = A_f32[0:4, T + 512:T + 1024]
    lns3 = A_f32[0:4, T + 1024:T + 1536]


    def Ebuf(i):
        i = i % NE
        return xln_f[:, i * 512:i * 512 + 512]

    def E2buf(i):
        i = (2 * i) % NE
        return xln_f[:, i * 512:i * 512 + 1024]
    sq_base = NE * 512
    sqscr = xln_f[:, sq_base:sq_base + T]
    f32b = sq_base // 2
    ssbA = xln_f32[:, f32b:f32b + 512]
    ssbB = xln_f32[:, f32b + 512:f32b + 1024]
    if f32b + 1024 + T <= NCH * T // 2:
        zrow = xln_f32[0:1, f32b + 1024:f32b + 1024 + T]
    else:
        zrow = xln_f32[0:1, f32b:f32b + T]
    assert f32b + 1024 <= NCH * T // 2
    assert sq_base + T <= NCH * T

    ring = Ring(p, ring_t[:], nslots=6)
    pb03 = [nc.psum_tensor(f"psb{i}", [128, 512], F32).__enter__() for i in range(4)]
    pSA_t = nc.psum_tensor("psA", [128, 1024], F32).__enter__()
    pSB_t = nc.psum_tensor("psB", [128, 1024], F32).__enter__()
    pSA = pSA_t[:]
    pSB = pSB_t[:]
    pbk = [t[:] for t in pb03] + [pSA[:, 0:512], pSA[:, 512:1024],
                                  pSB[:, 0:512], pSB[:, 512:1024]]

    C_GS0 = 0
    C_GSR = 4
    C_POL = 8
    C_PMX = 12
    C_PS = 16
    C_PSR = 20
    C_SMX = 24
    C_NB8 = 28
    C_Z = 32
    C_ZR = 36
    C_QA = 40
    C_KA = 41
    C_BA = 42
    C_QB = 44
    C_KB = 45
    C_BB = 46
    C_G = 48
    C_GM = 52
    C_BT = 56
    C_PS2 = 60

    for dst, src in [(mask_t[:], mask_h), (colsel_t[:], colsel_h),
                     (aqb_t[:], aqb_h), (akb_t[:], akb_h), (avb_t[:], avb_h),
                     (aob_t[:], aob_h), (bqb_t[:], bqb_h), (bkb_t[:], bkb_h),
                     (pb1_t[:], pb1_h), (pb2_t[:], pb2_h), (sb1_t[:], sb1_h),
                     (sb2_t[:], sb2_h)]:
        p.dma("sync", dst, src, qx)
    smalls_ld = qx.n
    p.op("vector", "memset", ones64_t[:], 1.0, inc=ve)
    p.op("vector", "memset", negf_t[:], -1e30, inc=ve)
    p.op("vector", "memset", tiny_t[:], 1.0)
    p.op("vector", "memset", g48_t[:], 0.0)
    p.op("vector", "memset", hg_t[:], 0.0)
    p.op("vector", "memset", srow_t[:], 0.0)
    p.wait("vector", qx, smalls_ld)
    p.op("vector", "tensor_copy", avbb_t[:], avb_t[:], inc=ve)
    p.wait("tensor", qx, smalls_ld)
    p.wait("vector", qx, smalls_ld)
    p.wait("scalar", qx, smalls_ld)


    def rep_ap(src_row, P):
        from concourse.ap import AP as _AP
        a = src_row.ap
        assert a[0][1] == 1
        return _AP(src_row.tensor, src_row.offset,
                   [[a[0][0], 1], [0, P]] + [list(x) for x in a[1:]])

    def bcast(dst, src_row, queue="scalar", sem=None):
        s = sem or ch
        width = 1
        for d in src_row.shape[1:]:
            width *= d
        if width <= 64:
            p.dma(queue, dst, rep_ap(src_row, dst.shape[0]), s)
            return
        p.dma(queue, dst[0:16], rep_ap(src_row, 16), s)
        k = 16
        while k < dst.shape[0]:
            p.wait(queue, s)
            p.dma(queue, dst[k:2 * k], dst[0:k], s)
            k *= 2

    class LNState:
        def __init__(self, scr, guard):
            self.sq_se = []
            self.stats_te = []
            self.scr = scr
            self.guard = guard

    def ln_begin(scr=None, guard=True):
        if scr is None:
            scr = lambda c: A_t[:, c % 2, :]
        return LNState(scr, guard)

    def _ln_stats(st, c):
        if c == 0:
            banks.use(0, "tensor")
            banks.use(1, "tensor")
        sq = st.scr(c)
        p.wait("tensor", se, st.sq_se[c])
        for j in range(NT):
            sl = slice(j * 512, j * 512 + 512)
            p.op("tensor", "matmul", pbk[0][0:4, :], colsel_t[:, j, 0:4],
                 xln_t[:, c, sl], start=(c == 0 and j == 0),
                 stop=(c == NCH - 1 and j == NT - 1))
            p.op("tensor", "matmul", pbk[1][0:4, :], colsel_t[:, j, 0:4],
                 sq[:, sl], start=(c == 0 and j == 0),
                 stop=(c == NCH - 1 and j == NT - 1),
                 inc=(te if j == NT - 1 else None))
        st.stats_te.append(te.n)

    def ln_chunk(st, c, res=None):
        sq = st.scr(c)
        if res is not None:
            p.wait("scalar", res[0], res[1])
        if st.guard and c >= 2:
            p.wait("scalar", te, st.stats_te[c - 2])
        p.op("scalar", "activation", xln_t[:, c, :], xres_t[:, c, :],
             AF.Identity, inc=se)
        p.op("scalar", "activation", sq, xres_t[:, c, :], AF.Square, inc=se)
        st.sq_se.append(se.n)
        if c >= 1:
            _ln_stats(st, c - 1)

    def ln_finish(st, first=False):
        _ln_stats(st, NCH - 1)
        stx = pbk[0]
        sxx = pbk[1]
        p.wait("vector", te)
        p.op("vector", "tensor_scalar", lnm[0:NT, :], stx[0:NT, :],
             1.0 / D, None, ALU.mult, inc=ve)
        banks.done(0, ve)
        p.op("vector", "tensor_scalar", lns3[0:NT, :], sxx[0:NT, :],
             1.0 / D, EPS, ALU.mult, ALU.add, inc=ve)
        banks.done(1, ve)
        p.wait("gpsimd", ve)
        p.op("gpsimd", "tensor_mul", lns2[0:NT, :], lnm[0:NT, :], lnm[0:NT, :], inc=ge)
        p.op("gpsimd", "tensor_sub", lns3[0:NT, :], lns3[0:NT, :], lns2[0:NT, :], inc=ge)
        p.wait("vector", ge)
        p.op("vector", "reciprocal", lns2[0:NT, :], lns3[0:NT, :], inc=ve)
        p.wait("scalar", ve)
        p.op("scalar", "activation", lns3[0:NT, :], lns2[0:NT, :], AF.Sqrt, inc=se)
        p.wait("vector", se)
        p.op("vector", "tensor_mul", lns2[0:NT, :], lnm[0:NT, :], lns3[0:NT, :], inc=ve)
        p.op("scalar", "activation", rrow_bf[0:NT, :], lns3[0:NT, :],
             AF.Identity, inc=se)
        p.wait("vector", se)
        p.op("vector", "tensor_copy", mrrow_bf[0:NT, :], lns2[0:NT, :], inc=ve)
        p.wait("scalar", se)
        p.wait("scalar", ve)
        p.wait("sync", ve)
        p.dma("scalar", R_bc[0:1, :].rearrange("p (a b) -> p a b", b=512),
              rrow_bf[0:NT, :], ch)
        p.dma("sync", MR_bc[0:1, :].rearrange("p (a b) -> p a b", b=512),
              mrrow_bf[0:NT, :], qx)
        p.wait("scalar", ch)
        p.wait("sync", qx)
        p.dma("scalar", R_bc[1:16], rep_ap(R_bc[0:1, :], 15), ch)
        p.dma("sync", MR_bc[1:16], rep_ap(MR_bc[0:1, :], 15), qx)
        k = 16
        while k < 128:
            p.wait("scalar", ch)
            p.wait("sync", qx)
            p.dma("scalar", R_bc[k:2 * k], R_bc[0:k], ch)
            p.dma("sync", MR_bc[k:2 * k], MR_bc[0:k], qx)
            k *= 2
        p.wait("vector", ch)
        p.wait("vector", qx)
        aff = []
        for c in range(NCH):
            p.op("vector", "tensor_mul", tpass, xln_t[:, c, :], R_bc, inc=ve)
            if first:
                p.op("vector", "scalar_tensor_tensor", xln_t[:, c, :], tpass,
                     1.0, MR_bc, op0=ALU.mult, op1=ALU.subtract, inc=ve,
                     accum_out=xm_t[:, c:c + 1])
            else:
                p.op("vector", "tensor_sub", xln_t[:, c, :], tpass, MR_bc,
                     inc=ve)
            aff.append(ve.n)
        return aff

    def proj_fm(w_pieces_h, bias_t, dst_t, aff, scale_t=None, chase=False):
        if not chase:
            p.wait("tensor", ve, aff[-1])
        for oc in range(NCH):
            pc, tok = ring.load(w_pieces_h[oc], 1, shape=(8, 128))
            ring.use(tok)
            bset = (oc % 2) * 4
            for b in range(bset, bset + NT):
                banks.use(b, "tensor")
            for kc in range(NCH):
                if chase and oc == 0:
                    p.wait("tensor", ve, aff[kc])
                for tt in range(NT):
                    bank = bset + tt
                    p.op("tensor", "matmul", pbk[bank][:, 0:TTW],
                         pc[:, kc, :], xln_t[:, kc, tt * 512:tt * 512 + TTW],
                         start=(kc == 0), stop=(kc == NCH - 1),
                         inc=(te if (kc == NCH - 1 and tt == NT - 1) else None))
            ring.free(tok, te)
            p.wait("scalar", te)
            for tt in range(NT):
                bank = bset + tt
                kw = {}
                if scale_t is not None:
                    kw["scale"] = scale_t[:, oc:oc + 1]
                p.op("scalar", "activation",
                     dst_t[:, oc, tt * 512:tt * 512 + TTW],
                     pbk[bank][:, 0:TTW], AF.Identity,
                     bias=bias_t[:, oc:oc + 1], inc=se, **kw)
                banks.done(bank, se)

    def proj_tm(w_col_h, aff, bias_bcast=None):
        p.wait("tensor", ve, aff[-1])
        for colh in range(2):
            pc, tok = ring.load(w_col_h[colh], 4, shape=(8, 512))
            ring.use(tok)
            for tc in range(NKC):
                bank = tc % 4
                banks.use(bank, "tensor")
                for kc in range(NCH):
                    p.op("tensor", "matmul", pbk[bank][:],
                         xln_t[:, kc, tc * 128:tc * 128 + 128], pc[:, kc, :],
                         start=(kc == 0), stop=(kc == NCH - 1),
                         inc=(te if kc == NCH - 1 else None))
                dst = C_t[:, tc, colh * 512:colh * 512 + 512]
                if bias_bcast is None:
                    p.wait("scalar", te)
                    p.op("scalar", "activation", dst, pbk[bank][:], AF.Identity, inc=se)
                    banks.done(bank, se)
                else:
                    p.wait("vector", te)
                    p.op("vector", "tensor_add", dst, pbk[bank][:],
                         bias_bcast[:, colh * 512:colh * 512 + 512], inc=ve)
                    banks.done(bank, ve)
            ring.free(tok, te)

    def bounds(src_t, dst_col):
        bank = 6
        banks.use(bank, "tensor")
        st = pbk[bank]
        p.wait("scalar", te)
        sq2 = xln_f[:, 0:T]
        sq_te = []
        for c in range(NCH):
            sq = sqscr if c % 2 == 0 else sq2
            if c >= 2:
                p.wait("scalar", te, sq_te[c - 2])
            p.op("scalar", "activation", sq,
                 src_t[:, c, :], AF.Square, inc=se)
            p.wait("tensor", se)
            for j in range(NT):
                p.op("tensor", "matmul", st[0:8, :], colsel_t[:, j, :],
                     sq[:, j * 512:j * 512 + 512],
                     start=(c == 0 and j == 0),
                     stop=(c == NCH - 1 and j == NT - 1),
                     inc=(te if j == NT - 1 else None))
            sq_te.append(te.n)
        p.wait("vector", te)
        if dbg_stop == "bounds":
            dstage = xln_f32[:, 1280:1792]
            p.op("vector", "tensor_copy", dstage[0:4, :], st[0:4, :], inc=ve)
            p.wait("sync", ve)
            p.dma("sync", dbgN_h, dstage[0:4, :], qo)
            p.dma("sync", dbgQ_h[:, 0:T], sqscr, qo)
        p.op("vector", "reduce_max", tiny_t[0:NT, C_BT:C_BT + 1], st[0:NT, :], AX.X, inc=ve)
        banks.done(bank, ve)
        p.wait("scalar", ve)
        p.wait("scalar", ch)
        p.dma("scalar", tiny_t[0:1, C_G:C_G + NT], tiny_t[0:NT, C_BT:C_BT + 1], ch)
        p.wait("vector", ch)
        p.op("vector", "reduce_max", tiny_t[0:1, C_GM:C_GM + 1],
             tiny_t[0:1, C_G:C_G + NT], AX.X, inc=ve)
        p.wait("scalar", ve)
        p.op("scalar", "activation", tiny_t[0:1, dst_col:dst_col + 1],
             tiny_t[0:1, C_GM:C_GM + 1], AF.Sqrt, inc=se)


    def gating(aff):
        p.wait("vector", ve, aff[-1])
        p.op("vector", "tensor_scalar", xmb_t[:], xm_t[:], 1.0 / T, None,
             ALU.mult, inc=ve)
        xmb_ready = ve.n
        banks.use(5, "tensor")
        hidp = pbk[5]
        p.wait("tensor", ve, xmb_ready)
        for pp in range(3):
            pc, tok = ring.load(pw1_h[pp], 4, shape=(8, 512))
            ring.use(tok)
            for mt in range(4):
                col = pp * 4 + mt
                for kc in range(NCH):
                    last = (pp == 2 and mt == 3 and kc == NCH - 1)
                    p.op("tensor", "matmul", hidp[0:128, col:col + 1],
                         pc[:, kc, mt * 128:mt * 128 + 128],
                         xmb_t[:, kc:kc + 1],
                         start=(kc == 0 and mt == 0 and pp == 0), stop=last,
                         inc=(te if last else None))
            ring.free(tok, te)
        p.wait("vector", te)
        p.op("vector", "tensor_add", hid_t[:], hidp[0:128, 0:12], pb1_t[:], inc=ve)
        banks.done(5, ve)
        p.wait("scalar", ve)
        p.op("scalar", "activation", hidb_t[:], hid_t[:], AF.Gelu, inc=se)
        banks.use(5, "tensor")
        pc, tok = ring.load(sw1_h, 4, shape=(8, 512))
        ring.use(tok)
        p.wait("tensor", se)
        for mt in range(4):
            for kc in range(NCH):
                last = (mt == 3 and kc == NCH - 1)
                p.op("tensor", "matmul", hidp[0:128, mt:mt + 1],
                     pc[:, kc, mt * 128:mt * 128 + 128], xmb_t[:, kc:kc + 1],
                     start=(kc == 0 and mt == 0), stop=last,
                     inc=(te if last else None))
        ring.free(tok, te)
        p.wait("vector", te)
        p.op("vector", "tensor_add", phid_t[:], hidp[0:128, 0:4], sb1_t[:], inc=ve)
        banks.done(5, ve)
        p.wait("scalar", ve)
        p.op("scalar", "activation", phidb_t[:], phid_t[:], AF.Gelu, inc=se)
        banks.use(4, "tensor")
        glp = pbk[4]
        p.wait("tensor", se)
        pc2, tok2 = ring.load(pw2_h, 1, shape=(12, 16))
        ring.use(tok2)
        for pp in range(3):
            for mt in range(4):
                last = (pp == 2 and mt == 3)
                p.op("tensor", "matmul", glp[0:16, pp:pp + 1],
                     pc2[:, pp * 4 + mt, :],
                     hidb_t[:, pp * 4 + mt:pp * 4 + mt + 1],
                     start=(pp == 0 and mt == 0), stop=last,
                     inc=(te if last else None))
        ring.free(tok2, te)
        banks.use(5, "tensor")
        plp = pbk[5]
        ps2, tok3 = ring.load(sw2_h, 1, shape=(4, 3))
        ring.use(tok3)
        for mt in range(4):
            p.op("tensor", "matmul", plp[0:3, 0:1], ps2[:, mt, :],
                 phidb_t[:, mt:mt + 1], start=(mt == 0), stop=(mt == 3),
                 inc=(te if mt == 3 else None))
        ring.free(tok3, te)
        p.wait("vector", te)
        p.op("vector", "tensor_add", eg_t[:], glp[0:16, 0:3], pb2_t[:], inc=ve)
        banks.done(4, ve)
        p.op("vector", "tensor_add", pl3_t[:], plp[0:3, 0:1], sb2_t[:], inc=ve)
        banks.done(5, ve)
        p.wait("scalar", ve)
        for pp in range(3):
            p.wait("scalar", ch)
            p.dma("scalar", g48_t[0:1, pp * 16:pp * 16 + 16],
                  eg_t[0:16, pp:pp + 1], ch)
        p.wait("scalar", ch)
        p.dma("scalar", tiny_t[0:1, C_POL:C_POL + 3], pl3_t[0:3, 0:1], ch)
        p.wait("vector", ch)
        g3x16 = g48_t[0:1, :].rearrange("p (a b) -> p a b", b=16)
        p.op("vector", "reduce_max", tiny_t[0:1, C_SMX:C_SMX + 3], g3x16, AX.X, inc=ve)
        p.wait("gpsimd", ve)
        for pp in range(3):
            p.op("gpsimd", "tensor_scalar", g48_t[0:1, pp * 16:pp * 16 + 16],
                 g48_t[0:1, pp * 16:pp * 16 + 16],
                 tiny_t[0:1, C_SMX + pp:C_SMX + pp + 1], None, ALU.subtract,
                 inc=ge)
        p.wait("scalar", ve)
        p.wait("scalar", ge)
        p.op("scalar", "activation", g48_t[0:1, :], g48_t[0:1, :], AF.Exp, inc=se)
        p.wait("vector", se)
        p.op("vector", "reduce_sum", tiny_t[0:1, C_GS0:C_GS0 + 3], g3x16, AX.X, inc=ve)
        p.op("vector", "reduce_max", tiny_t[0:1, C_PMX:C_PMX + 1],
             tiny_t[0:1, C_POL:C_POL + 3], AX.X, inc=ve)
        p.wait("gpsimd", ve)
        p.op("gpsimd", "tensor_scalar", tiny_t[0:1, C_POL:C_POL + 3],
             tiny_t[0:1, C_POL:C_POL + 3], tiny_t[0:1, C_PMX:C_PMX + 1],
             None, ALU.subtract, inc=ge)
        p.wait("scalar", ve)
        p.wait("scalar", ge)
        p.op("scalar", "activation", tiny_t[0:1, C_POL:C_POL + 3],
             tiny_t[0:1, C_POL:C_POL + 3], AF.Exp, inc=se)
        p.wait("vector", se)
        p.op("vector", "reduce_sum", tiny_t[0:1, C_PS:C_PS + 1],
             tiny_t[0:1, C_POL:C_POL + 3], AX.X, inc=ve)
        p.wait("gpsimd", ve)
        p.op("gpsimd", "tensor_copy", tiny_t[0:1, C_PS2:C_PS2 + 1],
             tiny_t[0:1, C_PS:C_PS + 1], inc=ge)
        p.wait("vector", ge)
        p.op("vector", "reciprocal", tiny_t[0:1, C_PSR:C_PSR + 4],
             tiny_t[0:1, C_PS2:C_PS2 + 4], inc=ve)
        p.op("vector", "reciprocal", tiny_t[0:1, C_GSR:C_GSR + 4],
             tiny_t[0:1, C_GS0:C_GS0 + 4], inc=ve)
        p.wait("gpsimd", ve)
        p.op("gpsimd", "tensor_scalar", tiny_t[0:1, C_POL:C_POL + 3],
             tiny_t[0:1, C_POL:C_POL + 3], tiny_t[0:1, C_PSR:C_PSR + 1],
             None, ALU.mult, inc=ge)
        p.op("gpsimd", "tensor_mul", tiny_t[0:1, C_POL:C_POL + 3],
             tiny_t[0:1, C_POL:C_POL + 3], tiny_t[0:1, C_GSR:C_GSR + 3], inc=ge)
        p.op("gpsimd", "tensor_scalar", hg_t[0:1, :], g48_t[0:1, 0:16],
             tiny_t[0:1, C_POL:C_POL + 1], None, ALU.mult, inc=ge)
        for pp in (1, 2):
            p.op("gpsimd", "tensor_scalar", g48_t[0:1, pp * 16:pp * 16 + 16],
                 g48_t[0:1, pp * 16:pp * 16 + 16],
                 tiny_t[0:1, C_POL + pp:C_POL + pp + 1], None, ALU.mult, inc=ge)
            p.op("gpsimd", "tensor_add", hg_t[0:1, :], hg_t[0:1, :],
                 g48_t[0:1, pp * 16:pp * 16 + 16], inc=ge)
        p.op("gpsimd", "tensor_scalar", srow_t[0:1, 0:16], hg_t[0:1, :],
             1.0 / 8.0, None, ALU.mult, inc=ge)
        p.wait("scalar", ge)
        bcast(gsc_t[:], srow_t[0:1, 0:16])
        p.wait("vector", ch)
        for oc in range(NCH):
            p.op("vector", "tensor_copy", qsc_t[0:64, oc:oc + 1],
                 gsc_t[0:64, 2 * oc:2 * oc + 1])
            p.op("vector", "tensor_copy", qsc_t[64:128, oc:oc + 1],
                 gsc_t[64:128, 2 * oc + 1:2 * oc + 2], inc=ve)
        p.op("vector", "tensor_mul", bqg_t[:], aqb_t[:], qsc_t[:], inc=ve)
        return ve.n

    def attn_core_a(gates_ready):
        half = NCH * T // 2
        if T >= 2048:
            sring = [xln_f32[:, f32b:f32b + 512],
                     xln_f32[:, f32b + 512:f32b + 1024],
                     xln_f32[:, f32b + 1024:f32b + 1536],
                     xln_f32[:, half - 1024:half - 512]]
            rscr = xln_f32[:, half - 512:half]
        else:
            sring = [dvs_t[:, 0:512], dvs_t[:, 512:1024]]
            rscr = dvs_t[:, 1024:1536]
        eslot = 0
        pidx = 0
        dq = []

        def emit_S(qt, pr, kc):
            q0 = qt * 512
            w = TTW
            pS = pSA if kc % 2 == 0 else pSB
            b0 = 4 if kc % 2 == 0 else 6
            banks.use(b0, "tensor")
            banks.use(b0 + 1, "tensor")
            ksl = slice(kc * 128, kc * 128 + 128)
            qsl = slice(q0, q0 + w)
            p.op("tensor", "matmul", pS[:, 0:w],
                 B_t[0:64, pr, ksl], A_t[0:64, pr, qsl],
                 start=True, stop=True, tile_position=(0, 0))
            p.op("tensor", "matmul", pS[:, 512:512 + w],
                 B_t[64:128, pr, ksl], A_t[64:128, pr, qsl],
                 start=True, stop=True, tile_position=(64, 0), inc=te)
            off = kc * 128 - q0
            if off >= 0:
                p.wait("vector", te)
                for base in (0, 512):
                    if off > 0:
                        p.op("vector", "tensor_add", pS[:, base:base + off],
                             pS[:, base:base + off], negf_t[:, 0:off])
                    p.op("vector", "tensor_add",
                         pS[:, base + off:base + off + 128],
                         pS[:, base + off:base + off + 128], mask_t[:],
                         inc=(ve if base == 512 else None))
            return off

        def defer_divide(pr, q0, w, ri):
            sv = se.n
            h = w // 2

            def r(base, lo, hi, first=False, sv=sv, ri=ri):
                def f():
                    if first:
                        p.wait("vector", se, sv)
                    p.op("vector", "reciprocal", rscr[base:base + 64, lo:hi],
                         sring[ri][base:base + 64, lo:hi], inc=ve)
                return f

            def m(base, pr=pr, q0=q0, w=w):
                def f():
                    p.op("vector", "tensor_mul",
                         A_t[base:base + 64, pr, q0:q0 + w],
                         A_t[base:base + 64, pr, q0:q0 + w],
                         rscr[base:base + 64, 0:w], inc=ve)
                return f
            dq.extend([r(0, 0, h, True), r(0, h, w), m(0),
                       r(64, 0, h), r(64, h, w), m(64)])

        for qt in range(NT):
            q0 = qt * 512
            w = TTW
            nkc = (q0 + w) // 128
            for pr in range(NCH):
                for b in range(4):
                    banks.use(b, "tensor")
                off = emit_S(qt, pr, 0)
                offs = {0: off}
                e2s = {}
                for kc in range(nkc):
                    pS = pSA if kc % 2 == 0 else pSB
                    b0 = 4 if kc % 2 == 0 else 6
                    e2 = E2buf(eslot)
                    kE = eslot % (NE // 2)
                    eslot += 1
                    pass
                    esl.use(kE, "scalar")
                    p.wait("scalar", te)
                    if offs[kc] >= 0:
                        p.wait("scalar", ve)
                    p.wait("scalar", ch, gates_ready)
                    p.op("scalar", "activation", e2[:, 0:512 + w],
                         pS[:, 0:512 + w], AF.Exp, bias=nba_t[:], inc=se)
                    e2s[kc] = (e2, kE, se.n)
                    banks.done(b0, se)
                    banks.done(b0 + 1, se)
                    if dq:
                        dq.pop(0)()
                    if kc + 1 < nkc:
                        offs[kc + 1] = emit_S(qt, pr, kc + 1)
                    for kv in ([kc - 1] if kc >= 1 else []) + ([kc] if kc == nkc - 1 else []):
                        ev, kEv, sev = e2s.pop(kv)
                        p.wait("tensor", se, sev)
                        st = (kv == 0)
                        sp = (kv == nkc - 1)
                        p.op("tensor", "matmul", pbk[0][0:64, 0:w],
                             C_t[:, kv, pr * 128:pr * 128 + 64], ev[:, 0:w],
                             start=st, stop=sp, tile_position=(0, 0))
                        p.op("tensor", "matmul", pbk[2][0:64, 0:w],
                             ones64_t[:], ev[:, 0:w],
                             start=st, stop=sp, tile_position=(0, 0))
                        p.op("tensor", "matmul", pbk[1][64:128, 0:w],
                             C_t[:, kv, pr * 128 + 64:pr * 128 + 128],
                             ev[:, 512:512 + w],
                             start=st, stop=sp, tile_position=(0, 64))
                        p.op("tensor", "matmul", pbk[3][64:128, 0:w],
                             ones64_t[:], ev[:, 512:512 + w],
                             start=st, stop=sp, tile_position=(0, 64), inc=te)
                        esl.done(kEv, te)
                while len(dq) > 2:
                    dq.pop(0)()
                ri = pidx % len(sring)
                pidx += 1
                p.wait("scalar", te)
                p.op("scalar", "activation", A_t[0:64, pr, q0:q0 + w],
                     pbk[0][0:64, 0:w], AF.Identity, inc=se)
                p.op("scalar", "activation", A_t[64:128, pr, q0:q0 + w],
                     pbk[1][64:128, 0:w], AF.Identity, inc=se)
                banks.done(0, se)
                banks.done(1, se)
                p.wait("vector", te)
                p.op("vector", "tensor_copy", sring[ri][0:64, 0:w],
                     pbk[2][0:64, 0:w], inc=ve)
                p.op("vector", "tensor_copy", sring[ri][64:128, 0:w],
                     pbk[3][64:128, 0:w], inc=ve)
                banks.done(2, ve)
                banks.done(3, ve)
                defer_divide(pr, q0, w, ri)
        while dq:
            dq.pop(0)()

    def wo_resid(seq):
        p.wait("tensor", ve)
        res = []
        for oc in range(NCH):
            pc, tok = ring.load(ao_h[oc], 1, shape=(8, 128))
            ring.use(tok)
            bset = (oc % 2) * 4
            if seq == 0:
                banks.use(bset, "tensor")
                for kc in range(NCH):
                    p.op("tensor", "matmul", pbk[bset][0:128, 0:1],
                         pc[:, kc, :], avbb_t[:, kc:kc + 1],
                         start=(kc == 0), stop=(kc == NCH - 1),
                         inc=(te if kc == NCH - 1 else None))
                p.wait("scalar", te)
                p.op("scalar", "activation", bopr_t[:, oc:oc + 1],
                     pbk[bset][0:128, 0:1], AF.Identity,
                     bias=aob_t[:, oc:oc + 1], inc=se)
                banks.done(bset, se)
                p.wait("vector", se)
            for b in range(bset, bset + NT):
                banks.use(b, "tensor")
            for kc in range(NCH):
                for tt in range(NT):
                    bank = bset + tt
                    p.op("tensor", "matmul", pbk[bank][:, 0:TTW],
                         pc[:, kc, :], A_t[:, kc, tt * 512:tt * 512 + TTW],
                         start=(kc == 0), stop=(kc == NCH - 1),
                         inc=(te if (kc == NCH - 1 and tt == NT - 1) else None))
            ring.free(tok, te)
            p.wait("vector", te)
            for tt in range(NT):
                bank = bset + tt
                p.op("vector", "scalar_tensor_tensor",
                     xres_t[:, oc, tt * 512:tt * 512 + TTW],
                     pbk[bank][:, 0:TTW], bopr_t[:, oc:oc + 1],
                     xres_t[:, oc, tt * 512:tt * 512 + TTW],
                     op0=ALU.add, op1=ALU.add, inc=ve)
                banks.done(bank, ve)
            res.append((ve, ve.n))
        return res


    def ffn(aff, stop_after_h=False, hook=None):
        nh_tiles = NFH * NT
        assert nh_tiles * 512 <= 3 * NCH * T

        def hsl(hc, tt):
            i = hc * NT + tt
            off = i * 512
            if off < NCH * T:
                return A_f[:, off:off + TTW]
            off -= NCH * T
            if off < NCH * T:
                return B_f[:, off:off + TTW]
            off -= NCH * T
            return C_f[:, off:off + TTW]
        h_ve = []
        for hc in range(NFH):
            gu, gtok = ring.load(fgu_h[hc], 2, shape=(16, 128))
            gp = gu[:, 0:8, :]
            up = gu[:, 8:16, :]
            utok = gtok
            ring.use(gtok)
            for tt in range(NT):
                gb = tt % 2
                ub = 2 + tt % 2
                banks.use(gb, "tensor")
                banks.use(ub, "tensor")
                for kc in range(NCH):
                    if hc == 0 and tt == 0:
                        p.wait("tensor", ve, aff[kc])
                    p.op("tensor", "matmul", pbk[gb][:, 0:TTW], gp[:, kc, :],
                         xln_t[:, kc, tt * 512:tt * 512 + TTW],
                         start=(kc == 0), stop=(kc == NCH - 1),
                         inc=(te if kc == NCH - 1 else None))
                for kc in range(NCH):
                    p.op("tensor", "matmul", pbk[ub][:, 0:TTW], up[:, kc, :],
                         xln_t[:, kc, tt * 512:tt * 512 + TTW],
                         start=(kc == 0), stop=(kc == NCH - 1),
                         inc=(te if kc == NCH - 1 else None))
                p.wait("scalar", te)
                p.op("scalar", "activation", hsl(hc, tt), pbk[gb][:, 0:TTW],
                     AF.Silu, scale=INV127, inc=se)
                banks.done(gb, se)
                p.wait("vector", se)
                p.op("vector", "tensor_mul", hsl(hc, tt), pbk[ub][:, 0:TTW],
                     hsl(hc, tt), inc=ve)
                banks.done(ub, ve)
            ring.free(gtok, te)
            h_ve.append(ve.n)
        if stop_after_h:
            return
        for oc in range(NCH):
            dtoks = []
            for j in range(3):
                dpj, dtok = ring.load(fd_h[oc][:, 8 * j:8 * j + 8, :], 2,
                                      shape=(8, 128))
                dtoks.append((dpj, dtok))
            for _, dtok in dtoks:
                ring.use(dtok)
            for tt in range(NT):
                bank = 4 + tt
                banks.use(bank, "tensor")
                for hc in range(NFH):
                    if oc == 0 and tt == 0:
                        p.wait("tensor", ve, h_ve[hc])
                    dpj = dtoks[hc // 8][0]
                    p.op("tensor", "matmul", pbk[bank][:, 0:TTW],
                         dpj[:, hc % 8, :], hsl(hc, tt),
                         start=(hc == 0), stop=(hc == NFH - 1),
                         inc=(te if hc == NFH - 1 else None))
                p.wait("vector", te)
                p.op("vector", "scalar_tensor_tensor",
                     xres_t[:, oc, tt * 512:tt * 512 + TTW],
                     pbk[bank][:, 0:TTW], INV127 * INV127,
                     xres_t[:, oc, tt * 512:tt * 512 + TTW],
                     op0=ALU.mult, op1=ALU.add, inc=ve)
                banks.done(bank, ve)
            if hook is not None:
                hook(oc)
            for _, dtok in dtoks:
                ring.free(dtok, te)


    def attn_b(aff, hook=None):
        proj_fm(bq_h, bqb_t, A_t, aff, chase=True)
        proj_fm(bk_h, bkb_t, B_t, aff)
        ring.barrier()
        ring.n = 4
        bvb = ring_t[:, 4 * 1024:6 * 1024].bitcast(F32)

        p.dma("scalar", bvb[0:1, :], bvrow_h, ch)
        p.wait("scalar", ch)
        bcast(bvb, bvb[0:1, :])
        p.wait("vector", ch)
        proj_tm(bv_h, aff, bias_bcast=bvb)
        bounds(A_t, C_QB)
        bounds(B_t, C_KB)
        p.wait("vector", se)
        p.op("vector", "tensor_mul", tiny_t[0:1, C_BB:C_BB + 1],
             tiny_t[0:1, C_QB:C_QB + 1], tiny_t[0:1, C_KB:C_KB + 1], inc=ve)
        p.wait("gpsimd", ve)
        p.op("gpsimd", "tensor_scalar", tiny_t[0:1, C_BB:C_BB + 1],
             tiny_t[0:1, C_BB:C_BB + 1], -1.0 / 32.0, None, ALU.mult, inc=ge)
        p.wait("scalar", ve)
        p.wait("scalar", ge)

        bcast(nbb_t[:], tiny_t[0:1, C_BB:C_BB + 1])
        nb_ready = ch.n
        for qt in range(NT):
            q0 = qt * 512
            w = TTW
            nkc = (q0 + w) // 128
            ebufs = []
            for kc in range(nkc):
                sbank = 4 + (kc % 2)
                banks.use(sbank, "tensor")
                ksl = slice(kc * 128, kc * 128 + 128)
                for dc in range(NCH):
                    p.op("tensor", "matmul", pbk[sbank][:, 0:w],
                         B_t[:, dc, ksl], A_t[:, dc, q0:q0 + w],
                         start=(dc == 0), stop=(dc == NCH - 1),
                         inc=(te if dc == NCH - 1 else None))
                off = kc * 128 - q0
                if off >= 0:
                    p.wait("vector", te)
                    p.op("vector", "tensor_add", pbk[sbank][:, off:off + 128],
                         pbk[sbank][:, off:off + 128], mask_t[:], inc=ve)
                eb = Ebuf(kc)
                ebufs.append(eb)
                esl.use(kc % NE, "scalar")
                p.wait("scalar", te)
                if off >= 0:
                    p.wait("scalar", ve)
                p.wait("scalar", ch, nb_ready)
                ecols = slice(max(0, off), w)
                p.op("scalar", "activation", eb[:, ecols], pbk[sbank][:, ecols],
                     AF.Exp, scale=1.0 / 32.0, bias=nbb_t[:], inc=se)
                banks.done(sbank, se)
                if off > 0:
                    esl.use(kc % NE, "vector")
                    p.op("vector", "memset", eb[:, 0:off], 0.0, inc=ve)
            e_se = se.n
            e_ve = ve.n
            p.wait("tensor", se, e_se)
            p.wait("tensor", ve, e_ve)
            banks.use(2, "tensor")
            for kc in range(nkc):
                p.op("tensor", "matmul", pbk[2][0:64, 0:w], ones64_t[:],
                     ebufs[kc][:, 0:w], start=(kc == 0), stop=(kc == nkc - 1),
                     inc=(te if kc == nkc - 1 else None))
            p.wait("scalar", te)
            p.op("scalar", "activation", zrow[0:1, q0:q0 + w],
                 pbk[2][0:1, 0:w], AF.Identity, inc=se)
            banks.done(2, se)
            for dc in range(NCH):
                bank = dc % 2
                banks.use(bank, "tensor")
                for kc in range(nkc):
                    p.op("tensor", "matmul", pbk[bank][:, 0:w],
                         C_t[:, kc, dc * 128:dc * 128 + 128],
                         ebufs[kc][:, 0:w],
                         start=(kc == 0), stop=(kc == nkc - 1),
                         inc=(te if kc == nkc - 1 else None))
                if dc == NCH - 1:
                    for kc in range(nkc):
                        esl.done(kc % NE, te)
                p.wait("scalar", te)
                p.op("scalar", "activation", A_t[:, dc, q0:q0 + w],
                     pbk[bank][:, 0:w], AF.Identity, inc=se)
                banks.done(bank, se)
        p.wait("vector", se)
        p.op("vector", "reduce_sum", tiny_t[0:1, C_Z:C_Z + 1],
             zrow[0:1, 0:T], AX.X, inc=ve)
        p.wait("gpsimd", ve)
        p.op("gpsimd", "tensor_copy", tiny_t[0:1, C_PS2:C_PS2 + 1],
             tiny_t[0:1, C_Z:C_Z + 1], inc=ge)
        p.wait("vector", ge)
        p.op("vector", "reciprocal", tiny_t[0:1, C_ZR:C_ZR + 4],
             tiny_t[0:1, C_PS2:C_PS2 + 4], inc=ve)
        p.wait("scalar", ve)

        bcast(rzb_t[:], tiny_t[0:1, C_ZR:C_ZR + 1])
        p.wait("vector", ch)
        p.wait("vector", se)
        for dc in range(NCH):
            for tt in range(NT):
                p.op("vector", "scalar_tensor_tensor",
                     xres_t[:, dc, tt * 512:tt * 512 + TTW],
                     A_t[:, dc, tt * 512:tt * 512 + TTW], rzb_t[:],
                     xres_t[:, dc, tt * 512:tt * 512 + TTW],
                     op0=ALU.mult, op1=ALU.add, inc=ve)
            if hook is not None:
                hook(dc)


    stores_before = 0
    for seq in range(n_dram_seqs):
        p.wait("sync", ve)
        p.wait("sync", qo, stores_before)
        res1 = []
        for c in range(NCH):
            s = qLs[c % 4]
            p.dma("sync", xres_t[:, c, :], x_h[seq, c], s)
            res1.append((s, s.n))

        p.wait("scalar", te)
        p.wait("scalar", ve)
        st = ln_begin()
        for c in range(NCH):
            ln_chunk(st, c, res1[c])
        xr = ln_finish(st, first=True)
        if dbg_stop == "ln1":
            break
        proj_fm(ak_h, akb_t, B_t, xr, chase=True)
        gq = gating(xr)
        if dbg_stop == "gating":
            break
        p.wait("scalar", ve, gq)
        proj_fm(aq_h, bqg_t, A_t, xr, scale_t=qsc_t)
        proj_tm(av_h, xr)
        if dbg_stop == "proj":
            break
        bounds(A_t, C_QA)
        bounds(B_t, C_KA)
        if dbg_stop == "bounds":
            break
        p.wait("vector", se)
        p.op("vector", "tensor_mul", tiny_t[0:1, C_BA:C_BA + 1],
             tiny_t[0:1, C_QA:C_QA + 1], tiny_t[0:1, C_KA:C_KA + 1], inc=ve)
        p.wait("gpsimd", ve)
        p.op("gpsimd", "tensor_scalar", tiny_t[0:1, C_NB8:C_NB8 + 1],
             tiny_t[0:1, C_BA:C_BA + 1], -1.0, None, ALU.mult, inc=ge)
        p.wait("scalar", ge)
        bcast(nba_t[:], tiny_t[0:1, C_NB8:C_NB8 + 1])
        gr = ch.n
        attn_core_a(gr)
        if dbg_stop == "attn_core":
            break
        res = wo_resid(seq)
        ring.barrier()
        if dbg_stop == "attn_a":
            break

        st = ln_begin(scr=lambda c: B_t[:, 6 + c % 2, :])
        for c in range(NCH):
            ln_chunk(st, c, res[c])
        xr = ln_finish(st)
        if dbg_stop == "ln2":
            break
        ffn(xr, stop_after_h=(dbg_stop == "ffn_h"))
        if dbg_stop == "ffn_h":
            break
        ring.barrier()
        if dbg_stop == "ffn1":
            break
        p.wait("scalar", te)
        p.wait("scalar", ve)
        st = ln_begin()
        for c in range(NCH):
            ln_chunk(st, c)
        xr = ln_finish(st)

        st = ln_begin(scr=lambda c: A_t[:, c, :], guard=False)
        attn_b(xr, hook=(lambda dc, s=st: ln_chunk(s, dc, (ve, ve.n))))
        ring.freed[4].append((ve, ve.n))
        ring.freed[5].append((ve, ve.n))
        ring.n = ring.nmax
        ring.barrier()
        if dbg_stop == "attn_b":
            break
        xr = ln_finish(st)

        ffn(xr)
        ring.barrier()

        p.wait("sync", ve)
        for c in range(NCH):
            p.dma("sync", out_h[seq, c], xres_t[:, c, :], qo)
        stores_before = qo.n

    if dbg_stop:
        p.wait("sync", ve)
        p.wait("sync", se)
        p.wait("sync", te)
        p.wait("sync", ch)
        p.dma("sync", dbgA_h, A_t[:], qo)
        p.dma("sync", dbgB_h, B_t[:], qo)
        p.dma("sync", dbgC_h, C_t[:], qo)
        p.dma("sync", dbgX_h, xln_t[:], qo)
        p.dma("sync", dbgR_h, xres_t[:], qo)
        p.dma("sync", dbgS_h[:, 0:16], gsc_t[:], qo)
        p.dma("sync", dbgS_h[:, 16:32], gsc_t[:], qo)
        p.dma("sync", dbgS_h[0:4, 32:96], tiny_t[:], qo)
        p.dma("sync", dbgS_h[0:1, 96:112], hg_t[:], qo)
        p.dma("sync", dbgS_h[0:1, 112:160], g48_t[:], qo)
        stores_before = qo.n
    p.wait("gpsimd", qo, stores_before)
    p.op("gpsimd", "engine_nop")

    with nc.Block() as block:
        p.build(block)
    return nc



def _prep_weights(inputs):
    f = np.float32
    g = {}

    def piece_oc(wT):
        return np.ascontiguousarray(
            wT.reshape(8, 128, 8, 128).transpose(2, 1, 0, 3))

    def piece_col(wT):
        return np.ascontiguousarray(
            wT.reshape(8, 128, 2, 512).transpose(2, 1, 0, 3))

    def pm(b):
        return np.ascontiguousarray(np.asarray(b, f).reshape(8, 128).T)

    wq = np.asarray(inputs["L0_attn_a_wq"], f)
    wk = np.asarray(inputs["L0_attn_a_wk"], f)
    wv = np.asarray(inputs["L0_attn_a_wv"], f)
    wo = np.asarray(inputs["L0_attn_a_wo"], f)
    g["aq"] = piece_oc(np.ascontiguousarray(wq.T))
    g["ak"] = piece_oc(np.ascontiguousarray(wk.T))
    g["ao"] = piece_oc(np.ascontiguousarray(wo.T))
    g["av"] = piece_col(np.ascontiguousarray(wv.T))
    g["aqb"] = pm(inputs["L0_attn_a_bq"])
    g["akb"] = pm(inputs["L0_attn_a_bk"])
    g["avb"] = pm(inputs["L0_attn_a_bv"])
    g["aob"] = pm(inputs["L0_attn_a_bo"])
    pw1 = np.asarray(inputs["L0_attn_a_pw1"], f)
    g["pw1t"] = np.ascontiguousarray(
        pw1.transpose(0, 2, 1).reshape(3, 8, 128, 512).transpose(0, 2, 1, 3))
    pb1 = np.asarray(inputs["L0_attn_a_pb1"], f)
    g["pb1"] = np.ascontiguousarray(
        pb1.reshape(3, 4, 128).transpose(2, 0, 1).reshape(128, 12))
    pw2 = np.asarray(inputs["L0_attn_a_pw2"], f)
    g["pw2t"] = np.ascontiguousarray(
        pw2.transpose(0, 2, 1).reshape(3, 4, 128, 16)
        .transpose(2, 0, 1, 3).reshape(128, 192))
    g["pb2t"] = np.ascontiguousarray(np.asarray(inputs["L0_attn_a_pb2"], f).T)
    sw1 = np.asarray(inputs["L0_attn_a_sw1"], f)
    g["sw1t"] = np.ascontiguousarray(
        sw1.T.reshape(8, 128, 512).transpose(1, 0, 2))
    g["sb1"] = np.ascontiguousarray(
        np.asarray(inputs["L0_attn_a_sb1"], f).reshape(4, 128).T)
    sw2 = np.asarray(inputs["L0_attn_a_sw2"], f)
    g["sw2t"] = np.ascontiguousarray(
        sw2.T.reshape(4, 128, 3).transpose(1, 0, 2).reshape(128, 12))
    g["sb2"] = np.ascontiguousarray(
        np.asarray(inputs["L0_attn_a_sb2"], f).reshape(3, 1))
    Wq = np.asarray(inputs["L0_attn_b_Wq"], f)
    Wk = np.asarray(inputs["L0_attn_b_Wk"], f)
    Wv = np.asarray(inputs["L0_attn_b_Wv"], f)
    g["bq"] = piece_oc(Wq)
    g["bk"] = piece_oc(Wk)
    g["bv"] = piece_col(Wv)
    g["bqb"] = pm(inputs["L0_attn_b_bq"])
    g["bkb"] = pm(inputs["L0_attn_b_bk"])
    g["bvrow"] = np.ascontiguousarray(
        np.asarray(inputs["L0_attn_b_bv"], f).reshape(1, 1024))
    wg = np.asarray(inputs["L0_ffn_w_gate"], np.int32)
    wu = np.asarray(inputs["L0_ffn_w_up"], np.int32)
    wd = np.asarray(inputs["L0_ffn_w_down"], np.int32)
    fg = wg.T.reshape(8, 128, 24, 128).transpose(2, 1, 0, 3)
    fu = wu.T.reshape(8, 128, 24, 128).transpose(2, 1, 0, 3)
    g["fgu"] = np.ascontiguousarray(np.concatenate([fg, fu], axis=2))
    g["fd"] = np.ascontiguousarray(
        wd.T.reshape(24, 128, 8, 128).transpose(2, 1, 0, 3))
    g["mask"] = np.where(np.arange(128)[:, None] > np.arange(128)[None, :],
                         -1e30, 0.0).astype(BF)
    cs = np.zeros((128, 8, 8), BF)
    for j in range(8):
        cs[:, j, j] = 1.0
    g["colsel"] = cs
    return g


def kernel(**inputs):
    T = 2048
    n_cores = 8
    spc = 2
    x = np.asarray(inputs["L0_attn_a_x"], np.float32)
    g = _prep_weights(inputs)
    nc = build_program(T=T, n_dram_seqs=spc)
    in_maps = []
    for c in range(n_cores):
        m = dict(g)
        xs = np.empty((spc, NCH, 128, T), np.float32)
        for s in range(spc):
            xs[s] = x[c * spc + s].T.reshape(NCH, 128, T)
        m["x"] = xs
        in_maps.append(m)
    res = run_bass_kernel_spmd(nc, in_maps, core_ids=list(range(n_cores)))
    out = np.empty_like(x)
    for c in range(n_cores):
        o = np.asarray(res.results[c]["out"])
        for s in range(spc):
            out[c * spc + s] = o[s].reshape(D, T).T
    return out


# version 12
# speedup vs baseline: 1.5285x; 1.0205x over previous
import numpy as np
import ml_dtypes
import concourse.bass as bass
import concourse.mybir as mybir
from concourse.bass_utils import run_bass_kernel_spmd

F32 = mybir.dt.float32
BF16 = mybir.dt.bfloat16
I32 = mybir.dt.int32
AF = mybir.ActivationFunctionType
ALU = mybir.AluOpType
AX = mybir.AxisListType
BF = ml_dtypes.bfloat16

ENGINES = ("sync", "vector", "scalar", "tensor", "gpsimd")
D = 1024
NCH = 8
NFH = 24
EPS = 1e-5
INV127 = float(np.float32(1.0 / 127.0))


class Sem:
    __slots__ = ("h", "n", "name")

    def __init__(self, handle, name=""):
        self.h = handle
        self.n = 0
        self.name = name


class Prog:
    def __init__(self, nc):
        self.nc = nc
        self.streams = {e: [] for e in ENGINES}
        self.nins = 0

    def sem(self, name):
        return Sem(self.nc.semaphore(name).__enter__(), name)

    def emit(self, engine, fn):
        self.streams[engine].append(fn)

    def wait(self, engine, sem, val=None):
        v = sem.n if val is None else val
        if v <= 0:
            return
        self.emit(engine, lambda eng, s=sem.h, v=v: eng.wait_ge(s, v))

    def op(self, engine, method, *args, inc=None, **kw):
        self.nins += 1
        if inc is not None:
            sem, k = inc if isinstance(inc, tuple) else (inc, 1)
            sem.n += k

            def fn(eng, m=method, a=args, kw=kw, s=sem.h, k=k):
                getattr(eng, m)(*a, **kw).then_inc(s, k)
        else:
            def fn(eng, m=method, a=args, kw=kw):
                getattr(eng, m)(*a, **kw)
        self.emit(engine, fn)

    def dma(self, engine, out, in_, sem):
        self.op(engine, "dma_start", inc=(sem, 16), out=out, in_=in_)

    def build(self, block):
        for e in ENGINES:
            stream = self.streams[e]
            if not stream:
                continue

            def runner(eng, stream=stream):
                for fn in stream:
                    fn(eng)
            getattr(block, e)(runner)


class Banks:

    def __init__(self, p):
        self.p = p
        self.pend = {i: [] for i in range(8)}

    def use(self, i, engine):
        for sem, val in self.pend[i]:
            self.p.wait(engine, sem, val)
        self.pend[i] = []

    def done(self, i, sem):
        self.pend[i].append((sem, sem.n))


class Slots:

    def __init__(self, p):
        self.p = p
        self.pend = {}

    def use(self, key, engine):
        for sem, val in self.pend.get(key, ()):
            self.p.wait(engine, sem, val)
        self.pend[key] = []

    def done(self, key, sem):
        self.pend.setdefault(key, []).append((sem, sem.n))


class Ring:

    def __init__(self, p, area_bf, nslots=8):
        self.p = p
        self.area = area_bf
        self.n = nslots
        self.nmax = nslots
        self.sems = [p.sem(f"wr{i}") for i in range(nslots)]
        self.freed = {i: [] for i in range(nslots)}
        self.next = 0

    def load(self, dram_ap, width_slots=1, shape=None, cols=None):
        if self.next + width_slots > self.n:
            self.next = 0
        i0 = self.next
        self.next += width_slots
        slots = list(range(i0, i0 + width_slots))
        for i in slots:
            for sem, val in self.freed[i]:
                self.p.wait("gpsimd", sem, val)
            self.freed[i] = []
        if cols is None and shape is not None:
            n = 1
            for d in shape:
                n *= d
            cols = n
        ncols = cols if cols is not None else width_slots * 1024
        flat = self.area[:, i0 * 1024:i0 * 1024 + ncols]
        view = flat
        if shape is not None:
            view = flat.rearrange("p (a b) -> p a b", b=shape[-1])
        sem = self.sems[i0]
        self.p.dma("gpsimd", flat[0:64], dram_ap[0:64], sem)
        self.p.dma("gpsimd", flat[64:128], dram_ap[64:128], sem)
        return view, (slots, sem, sem.n)

    def use(self, token, engine="tensor"):
        _, sem, val = token
        self.p.wait(engine, sem, val)

    def free(self, token, sem):
        slots, _, _ = token
        for i in slots:
            self.freed[i].append((sem, sem.n))

    def barrier(self):
        for i in range(self.nmax):
            for sem, val in self.freed[i]:
                self.p.wait("gpsimd", sem, val)
            self.freed[i] = []
        self.next = 0


def build_program(T=2048, n_dram_seqs=2, dbg_stop=None):
    NT = max(1, T // 512)
    TTW = min(512, T)
    NKC = T // 128
    NE = min(16, NKC)
    nc = bass.Bass("TRN2", target_bir_lowering=False, debug=False,
                   detect_race_conditions=False, dynamic_dma_scratch_size=16384)

    def din(name, shape, dtype=F32):
        return nc.dram_tensor(name, list(shape), dtype, kind="ExternalInput").ap()

    x_h = din("x", [n_dram_seqs, NCH, 128, T])
    out_h = nc.dram_tensor("out", [n_dram_seqs, NCH, 128, T], F32,
                           kind="ExternalOutput").ap()
    aq_h = din("aq", [8, 128, 8, 128])
    ak_h = din("ak", [8, 128, 8, 128])
    ao_h = din("ao", [8, 128, 8, 128])
    av_h = din("av", [2, 128, 8, 512])
    aqb_h = din("aqb", [128, 8])
    akb_h = din("akb", [128, 8])
    avb_h = din("avb", [128, 8])
    aob_h = din("aob", [128, 8])
    pw1_h = din("pw1t", [3, 128, 8, 512])
    pb1_h = din("pb1", [128, 12])
    pw2_h = din("pw2t", [128, 192])
    pb2_h = din("pb2t", [16, 3])
    sw1_h = din("sw1t", [128, 8, 512])
    sb1_h = din("sb1", [128, 4])
    sw2_h = din("sw2t", [128, 12])
    sb2_h = din("sb2", [3, 1])
    bq_h = din("bq", [8, 128, 8, 128])
    bk_h = din("bk", [8, 128, 8, 128])
    bv_h = din("bv", [2, 128, 8, 512])
    bqb_h = din("bqb", [128, 8])
    bkb_h = din("bkb", [128, 8])
    bvrow_h = din("bvrow", [1, 1024])
    fgu_h = din("fgu", [NFH, 128, 16, 128], I32)
    fd_h = din("fd", [8, 128, NFH, 128], I32)
    mask_h = din("mask", [128, 128], BF16)
    colsel_h = din("colsel", [128, 8, 8], BF16)
    if dbg_stop:
        dbgA_h = nc.dram_tensor("dbgA", [128, NCH, T], BF16, kind="ExternalOutput").ap()
        dbgB_h = nc.dram_tensor("dbgB", [128, NCH, T], BF16, kind="ExternalOutput").ap()
        dbgC_h = nc.dram_tensor("dbgC", [128, NKC, 1024], BF16, kind="ExternalOutput").ap()
        dbgX_h = nc.dram_tensor("dbgX", [128, NCH, T], BF16, kind="ExternalOutput").ap()
        dbgR_h = nc.dram_tensor("dbgR", [128, NCH, T], F32, kind="ExternalOutput").ap()
        dbgS_h = nc.dram_tensor("dbgS", [128, 192], F32, kind="ExternalOutput").ap()
        dbgN_h = nc.dram_tensor("dbgN", [4, 512], F32, kind="ExternalOutput").ap()
        dbgQ_h = nc.dram_tensor("dbgQ", [128, 2048], BF16, kind="ExternalOutput").ap()

    p = Prog(nc)
    banks = Banks(p)
    esl = Slots(p)

    te = p.sem("te")
    ve = p.sem("ve")
    se = p.sem("se")
    ge = p.sem("ge")
    qx = p.sem("qx")
    qo = p.sem("qo")
    ch = p.sem("ch")
    qLs = [p.sem(f"qL{i}") for i in range(4)]

    sb_keep = []

    def alloc(name, shape, dtype):
        t = nc.sbuf_tensor(name, list(shape), dtype).__enter__()
        sb_keep.append(t)
        return t

    xres_t = alloc("xres", [128, NCH, T], F32)
    xln_t = alloc("xln", [128, NCH, T], BF16)
    A_t = alloc("Abuf", [128, NCH, T], BF16)
    B_t = alloc("Bbuf", [128, NCH, T], BF16)
    C_t = alloc("Cbuf", [128, NKC, 1024], BF16)
    ring_t = alloc("wring", [128, 6 * 1024], BF16)
    mask_t = alloc("mask_s", [128, 128], BF16)
    colsel_t = alloc("colsel_s", [128, 8, 8], BF16)
    ones64_t = alloc("ones64", [128, 64], BF16)
    aqb_t = alloc("aqb_s", [128, 8], F32)
    akb_t = alloc("akb_s", [128, 8], F32)
    avb_t = alloc("avb_s", [128, 8], F32)
    aob_t = alloc("aob_s", [128, 8], F32)
    bopr_t = alloc("bopr", [128, 8], F32)
    avbb_t = alloc("avbb", [128, 8], BF16)
    bqb_t = alloc("bqb_s", [128, 8], F32)
    bkb_t = alloc("bkb_s", [128, 8], F32)
    pb1_t = alloc("pb1_s", [128, 12], F32)
    pb2_t = alloc("pb2_s", [16, 3], F32)
    sb1_t = alloc("sb1_s", [128, 4], F32)
    sb2_t = alloc("sb2_s", [3, 1], F32)
    xm_t = alloc("xm", [128, 8], F32)
    xmb_t = alloc("xmb", [128, 8], BF16)
    hid_t = alloc("hid", [128, 12], F32)
    hidb_t = alloc("hidb", [128, 12], BF16)
    phid_t = alloc("phid", [128, 4], F32)
    phidb_t = alloc("phidb", [128, 4], BF16)
    eg_t = alloc("eg", [16, 3], F32)
    pl3_t = alloc("pl3", [3, 1], F32)
    g48_t = alloc("g48", [1, 48], F32)
    hg_t = alloc("hg", [1, 16], F32)
    srow_t = alloc("srow", [1, 32], F32)
    gsc_t = alloc("gsc", [128, 16], F32)
    gbi_t = gsc_t
    qsc_t = alloc("qsc", [128, 8], F32)
    bqg_t = alloc("bqg", [128, 8], F32)
    nba_t = alloc("nba", [128, 1], F32)
    negf_t = alloc("negf", [128, 384], BF16)
    tiny_t = alloc("tiny", [4, 64], F32)
    nbb_t = alloc("nbb", [128, 1], F32)
    rzb_t = alloc("rzb", [128, 1], F32)
    silu_t = alloc("silu", [128, 512], BF16)
    silu2_t = silu_t
    dvs_t = alloc("dvs", [128, 1536], F32) if T < 2048 else None

    xln_f = xln_t[:].rearrange("p c t -> p (c t)")
    xln_f32 = xln_f.bitcast(F32)
    A_f = A_t[:].rearrange("p c t -> p (c t)")
    B_f = B_t[:].rearrange("p c t -> p (c t)")
    B_f32 = B_f.bitcast(F32)
    C_f = C_t[:].rearrange("p c t -> p (c t)")

    R_bc = B_f[:, 0:T]
    MR_bc = B_f[:, T:2 * T]
    tpass = B_f[:, 2 * T:3 * T]
    rrow_bf = B_f[0:4, 3 * T:3 * T + 512]
    mrrow_bf = B_f[0:4, 3 * T + 512:3 * T + 1024]
    A_f32 = A_f.bitcast(F32)
    lnm = A_f32[0:4, T:T + 512]
    lns2 = A_f32[0:4, T + 512:T + 1024]
    lns3 = A_f32[0:4, T + 1024:T + 1536]


    def Ebuf(i):
        i = i % NE
        return xln_f[:, i * 512:i * 512 + 512]

    def E2buf(i):
        i = (2 * i) % NE
        return xln_f[:, i * 512:i * 512 + 1024]
    sq_base = NE * 512
    sqscr = xln_f[:, sq_base:sq_base + T]
    f32b = sq_base // 2
    ssbA = xln_f32[:, f32b:f32b + 512]
    ssbB = xln_f32[:, f32b + 512:f32b + 1024]
    if f32b + 1024 + T <= NCH * T // 2:
        zrow = xln_f32[0:1, f32b + 1024:f32b + 1024 + T]
    else:
        zrow = xln_f32[0:1, f32b:f32b + T]
    assert f32b + 1024 <= NCH * T // 2
    assert sq_base + T <= NCH * T

    ring = Ring(p, ring_t[:], nslots=6)
    pb03 = [nc.psum_tensor(f"psb{i}", [128, 512], F32).__enter__() for i in range(4)]
    pSA_t = nc.psum_tensor("psA", [128, 1024], F32).__enter__()
    pSB_t = nc.psum_tensor("psB", [128, 1024], F32).__enter__()
    pSA = pSA_t[:]
    pSB = pSB_t[:]
    pbk = [t[:] for t in pb03] + [pSA[:, 0:512], pSA[:, 512:1024],
                                  pSB[:, 0:512], pSB[:, 512:1024]]

    C_GS0 = 0
    C_GSR = 4
    C_POL = 8
    C_PMX = 12
    C_PS = 16
    C_PSR = 20
    C_SMX = 24
    C_NB8 = 28
    C_Z = 32
    C_ZR = 36
    C_QA = 40
    C_KA = 41
    C_BA = 42
    C_QB = 44
    C_KB = 45
    C_BB = 46
    C_G = 48
    C_GM = 52
    C_BT = 56
    C_PS2 = 60

    for dst, src in [(mask_t[:], mask_h), (colsel_t[:], colsel_h),
                     (aqb_t[:], aqb_h), (akb_t[:], akb_h), (avb_t[:], avb_h),
                     (aob_t[:], aob_h), (bqb_t[:], bqb_h), (bkb_t[:], bkb_h),
                     (pb1_t[:], pb1_h), (pb2_t[:], pb2_h), (sb1_t[:], sb1_h),
                     (sb2_t[:], sb2_h)]:
        p.dma("sync", dst, src, qx)
    smalls_ld = qx.n
    p.op("vector", "memset", ones64_t[:], 1.0, inc=ve)
    p.op("vector", "memset", negf_t[:], -1e30, inc=ve)
    p.op("vector", "memset", tiny_t[:], 1.0)
    p.op("vector", "memset", g48_t[:], 0.0)
    p.op("vector", "memset", hg_t[:], 0.0)
    p.op("vector", "memset", srow_t[:], 0.0)
    p.wait("vector", qx, smalls_ld)
    p.op("vector", "tensor_copy", avbb_t[:], avb_t[:], inc=ve)
    p.wait("tensor", qx, smalls_ld)
    p.wait("vector", qx, smalls_ld)
    p.wait("scalar", qx, smalls_ld)


    def rep_ap(src_row, P):
        from concourse.ap import AP as _AP
        a = src_row.ap
        assert a[0][1] == 1
        return _AP(src_row.tensor, src_row.offset,
                   [[a[0][0], 1], [0, P]] + [list(x) for x in a[1:]])

    def bcast(dst, src_row, queue="scalar", sem=None):
        s = sem or ch
        width = 1
        for d in src_row.shape[1:]:
            width *= d
        if width <= 64:
            p.dma(queue, dst, rep_ap(src_row, dst.shape[0]), s)
            return
        p.dma(queue, dst[0:16], rep_ap(src_row, 16), s)
        k = 16
        while k < dst.shape[0]:
            p.wait(queue, s)
            p.dma(queue, dst[k:2 * k], dst[0:k], s)
            k *= 2

    class LNState:
        def __init__(self, scr, guard, depth):
            self.sq_se = []
            self.stats_te = []
            self.scr = scr
            self.guard = guard
            self.depth = depth

    def ln_begin(scr=None, guard=True, depth=4):
        if scr is None:
            scr = lambda c: A_t[:, c % 4, :]
        return LNState(scr, guard, depth)

    def _ln_stats(st, c):
        if c == 0:
            banks.use(0, "tensor")
            banks.use(1, "tensor")
        sq = st.scr(c)
        p.wait("tensor", se, st.sq_se[c])
        for j in range(NT):
            sl = slice(j * 512, j * 512 + 512)
            p.op("tensor", "matmul", pbk[0][0:4, :], colsel_t[:, j, 0:4],
                 xln_t[:, c, sl], start=(c == 0 and j == 0),
                 stop=(c == NCH - 1 and j == NT - 1))
            p.op("tensor", "matmul", pbk[1][0:4, :], colsel_t[:, j, 0:4],
                 sq[:, sl], start=(c == 0 and j == 0),
                 stop=(c == NCH - 1 and j == NT - 1),
                 inc=(te if j == NT - 1 else None))
        st.stats_te.append(te.n)

    def ln_chunk(st, c, res=None):
        sq = st.scr(c)
        if res is not None:
            p.wait("scalar", res[0], res[1])
        if st.guard and c >= st.depth:
            p.wait("scalar", te, st.stats_te[c - st.depth])
        p.op("scalar", "activation", xln_t[:, c, :], xres_t[:, c, :],
             AF.Identity, inc=se)
        p.op("scalar", "activation", sq, xres_t[:, c, :], AF.Square, inc=se)
        st.sq_se.append(se.n)
        if c >= 1:
            _ln_stats(st, c - 1)

    def ln_finish(st, first=False):
        _ln_stats(st, NCH - 1)
        stx = pbk[0]
        sxx = pbk[1]
        p.wait("vector", te)
        p.op("vector", "tensor_scalar", lnm[0:NT, :], stx[0:NT, :],
             1.0 / D, None, ALU.mult, inc=ve)
        banks.done(0, ve)
        p.op("vector", "tensor_scalar", lns3[0:NT, :], sxx[0:NT, :],
             1.0 / D, EPS, ALU.mult, ALU.add, inc=ve)
        banks.done(1, ve)
        p.wait("gpsimd", ve)
        p.op("gpsimd", "tensor_mul", lns2[0:NT, :], lnm[0:NT, :], lnm[0:NT, :], inc=ge)
        p.op("gpsimd", "tensor_sub", lns3[0:NT, :], lns3[0:NT, :], lns2[0:NT, :], inc=ge)
        p.wait("vector", ge)
        p.op("vector", "reciprocal", lns2[0:NT, :], lns3[0:NT, :], inc=ve)
        p.wait("scalar", ve)
        p.op("scalar", "activation", lns3[0:NT, :], lns2[0:NT, :], AF.Sqrt, inc=se)
        p.wait("vector", se)
        p.op("vector", "tensor_mul", lns2[0:NT, :], lnm[0:NT, :], lns3[0:NT, :], inc=ve)
        p.op("scalar", "activation", rrow_bf[0:NT, :], lns3[0:NT, :],
             AF.Identity, inc=se)
        p.wait("vector", se)
        p.op("vector", "tensor_copy", mrrow_bf[0:NT, :], lns2[0:NT, :], inc=ve)
        p.wait("scalar", se)
        p.wait("scalar", ve)
        p.wait("sync", ve)
        p.dma("scalar", R_bc[0:1, :].rearrange("p (a b) -> p a b", b=512),
              rrow_bf[0:NT, :], ch)
        p.dma("sync", MR_bc[0:1, :].rearrange("p (a b) -> p a b", b=512),
              mrrow_bf[0:NT, :], qx)
        p.wait("scalar", ch)
        p.wait("sync", qx)
        p.dma("scalar", R_bc[1:16], rep_ap(R_bc[0:1, :], 15), ch)
        p.dma("sync", MR_bc[1:16], rep_ap(MR_bc[0:1, :], 15), qx)
        k = 16
        while k < 128:
            p.wait("scalar", ch)
            p.wait("sync", qx)
            p.dma("scalar", R_bc[k:2 * k], R_bc[0:k], ch)
            p.dma("sync", MR_bc[k:2 * k], MR_bc[0:k], qx)
            k *= 2
        p.wait("vector", ch)
        p.wait("vector", qx)
        aff = []
        for c in range(NCH):
            p.op("vector", "tensor_mul", tpass, xln_t[:, c, :], R_bc, inc=ve)
            if first:
                p.op("vector", "scalar_tensor_tensor", xln_t[:, c, :], tpass,
                     1.0, MR_bc, op0=ALU.mult, op1=ALU.subtract, inc=ve,
                     accum_out=xm_t[:, c:c + 1])
            else:
                p.op("vector", "tensor_sub", xln_t[:, c, :], tpass, MR_bc,
                     inc=ve)
            aff.append(ve.n)
        return aff

    def proj_fm(w_pieces_h, bias_t, dst_t, aff, scale_t=None, chase=False,
                pre=None):
        if not chase:
            p.wait("tensor", ve, aff[-1])
        for oc in range(NCH):
            if pre is not None and oc < len(pre):
                pc, tok = pre[oc]
            else:
                pc, tok = ring.load(w_pieces_h[oc], 1, shape=(8, 128))
            ring.use(tok)
            bset = (oc % 2) * 4
            for b in range(bset, bset + NT):
                banks.use(b, "tensor")
            for kc in range(NCH):
                if chase and oc == 0:
                    p.wait("tensor", ve, aff[kc])
                for tt in range(NT):
                    bank = bset + tt
                    p.op("tensor", "matmul", pbk[bank][:, 0:TTW],
                         pc[:, kc, :], xln_t[:, kc, tt * 512:tt * 512 + TTW],
                         start=(kc == 0), stop=(kc == NCH - 1),
                         inc=(te if (kc == NCH - 1 and tt == NT - 1) else None))
            ring.free(tok, te)
            p.wait("scalar", te)
            for tt in range(NT):
                bank = bset + tt
                kw = {}
                if scale_t is not None:
                    kw["scale"] = scale_t[:, oc:oc + 1]
                p.op("scalar", "activation",
                     dst_t[:, oc, tt * 512:tt * 512 + TTW],
                     pbk[bank][:, 0:TTW], AF.Identity,
                     bias=bias_t[:, oc:oc + 1], inc=se, **kw)
                banks.done(bank, se)

    def proj_tm(w_col_h, aff, bias_bcast=None):
        p.wait("tensor", ve, aff[-1])
        for colh in range(2):
            pc, tok = ring.load(w_col_h[colh], 4, shape=(8, 512))
            ring.use(tok)
            for tc in range(NKC):
                bank = tc % 4
                banks.use(bank, "tensor")
                for kc in range(NCH):
                    p.op("tensor", "matmul", pbk[bank][:],
                         xln_t[:, kc, tc * 128:tc * 128 + 128], pc[:, kc, :],
                         start=(kc == 0), stop=(kc == NCH - 1),
                         inc=(te if kc == NCH - 1 else None))
                dst = C_t[:, tc, colh * 512:colh * 512 + 512]
                if bias_bcast is None:
                    p.wait("scalar", te)
                    p.op("scalar", "activation", dst, pbk[bank][:], AF.Identity, inc=se)
                    banks.done(bank, se)
                else:
                    p.wait("vector", te)
                    p.op("vector", "tensor_add", dst, pbk[bank][:],
                         bias_bcast[:, colh * 512:colh * 512 + 512], inc=ve)
                    banks.done(bank, ve)
            ring.free(tok, te)

    def bounds(src_t, dst_col):
        bank = 6
        banks.use(bank, "tensor")
        st = pbk[bank]
        p.wait("scalar", te)
        bscr = [sqscr, xln_f[:, 0:T], xln_f[:, T:2 * T], xln_f[:, 2 * T:3 * T]]
        sq_te = []
        for c in range(NCH):
            sq = bscr[c % 4]
            if c >= 4:
                p.wait("scalar", te, sq_te[c - 4])
            p.op("scalar", "activation", sq,
                 src_t[:, c, :], AF.Square, inc=se)
            p.wait("tensor", se)
            for j in range(NT):
                p.op("tensor", "matmul", st[0:8, :], colsel_t[:, j, :],
                     sq[:, j * 512:j * 512 + 512],
                     start=(c == 0 and j == 0),
                     stop=(c == NCH - 1 and j == NT - 1),
                     inc=(te if j == NT - 1 else None))
            sq_te.append(te.n)
        p.wait("vector", te)
        if dbg_stop == "bounds":
            dstage = xln_f32[:, 1280:1792]
            p.op("vector", "tensor_copy", dstage[0:4, :], st[0:4, :], inc=ve)
            p.wait("sync", ve)
            p.dma("sync", dbgN_h, dstage[0:4, :], qo)
            p.dma("sync", dbgQ_h[:, 0:T], sqscr, qo)
        p.op("vector", "reduce_max", tiny_t[0:NT, C_BT:C_BT + 1], st[0:NT, :], AX.X, inc=ve)
        banks.done(bank, ve)
        p.wait("scalar", ve)
        p.wait("scalar", ch)
        p.dma("scalar", tiny_t[0:1, C_G:C_G + NT], tiny_t[0:NT, C_BT:C_BT + 1], ch)
        p.wait("vector", ch)
        p.op("vector", "reduce_max", tiny_t[0:1, C_GM:C_GM + 1],
             tiny_t[0:1, C_G:C_G + NT], AX.X, inc=ve)
        p.wait("scalar", ve)
        p.op("scalar", "activation", tiny_t[0:1, dst_col:dst_col + 1],
             tiny_t[0:1, C_GM:C_GM + 1], AF.Sqrt, inc=se)


    def gating(aff, pre_h=None, npre=0):
        p.wait("vector", ve, aff[-1])
        p.op("vector", "tensor_scalar", xmb_t[:], xm_t[:], 1.0 / T, None,
             ALU.mult, inc=ve)
        xmb_ready = ve.n
        banks.use(5, "tensor")
        hidp = pbk[5]
        p.wait("tensor", ve, xmb_ready)
        for pp in range(3):
            pc, tok = ring.load(pw1_h[pp], 4, shape=(8, 512))
            ring.use(tok)
            for mt in range(4):
                col = pp * 4 + mt
                for kc in range(NCH):
                    last = (pp == 2 and mt == 3 and kc == NCH - 1)
                    p.op("tensor", "matmul", hidp[0:128, col:col + 1],
                         pc[:, kc, mt * 128:mt * 128 + 128],
                         xmb_t[:, kc:kc + 1],
                         start=(kc == 0 and mt == 0 and pp == 0), stop=last,
                         inc=(te if last else None))
            ring.free(tok, te)
        p.wait("vector", te)
        p.op("vector", "tensor_add", hid_t[:], hidp[0:128, 0:12], pb1_t[:], inc=ve)
        banks.done(5, ve)
        p.wait("scalar", ve)
        p.op("scalar", "activation", hidb_t[:], hid_t[:], AF.Gelu, inc=se)
        banks.use(5, "tensor")
        pc, tok = ring.load(sw1_h, 4, shape=(8, 512))
        ring.use(tok)
        p.wait("tensor", se)
        for mt in range(4):
            for kc in range(NCH):
                last = (mt == 3 and kc == NCH - 1)
                p.op("tensor", "matmul", hidp[0:128, mt:mt + 1],
                     pc[:, kc, mt * 128:mt * 128 + 128], xmb_t[:, kc:kc + 1],
                     start=(kc == 0 and mt == 0), stop=last,
                     inc=(te if last else None))
        ring.free(tok, te)
        p.wait("vector", te)
        p.op("vector", "tensor_add", phid_t[:], hidp[0:128, 0:4], sb1_t[:], inc=ve)
        banks.done(5, ve)
        p.wait("scalar", ve)
        p.op("scalar", "activation", phidb_t[:], phid_t[:], AF.Gelu, inc=se)
        banks.use(4, "tensor")
        glp = pbk[4]
        p.wait("tensor", se)
        pc2, tok2 = ring.load(pw2_h, 1, shape=(12, 16))
        ring.use(tok2)
        for pp in range(3):
            for mt in range(4):
                last = (pp == 2 and mt == 3)
                p.op("tensor", "matmul", glp[0:16, pp:pp + 1],
                     pc2[:, pp * 4 + mt, :],
                     hidb_t[:, pp * 4 + mt:pp * 4 + mt + 1],
                     start=(pp == 0 and mt == 0), stop=last,
                     inc=(te if last else None))
        ring.free(tok2, te)
        banks.use(5, "tensor")
        plp = pbk[5]
        ps2, tok3 = ring.load(sw2_h, 1, shape=(4, 3))
        pretoks = [ring.load(pre_h[i], 1, shape=(8, 128)) for i in range(npre)]
        ring.use(tok3)
        for mt in range(4):
            p.op("tensor", "matmul", plp[0:3, 0:1], ps2[:, mt, :],
                 phidb_t[:, mt:mt + 1], start=(mt == 0), stop=(mt == 3),
                 inc=(te if mt == 3 else None))
        ring.free(tok3, te)
        p.wait("vector", te)
        p.op("vector", "tensor_add", eg_t[:], glp[0:16, 0:3], pb2_t[:], inc=ve)
        banks.done(4, ve)
        p.op("vector", "tensor_add", pl3_t[:], plp[0:3, 0:1], sb2_t[:], inc=ve)
        banks.done(5, ve)
        p.wait("scalar", ve)
        for pp in range(3):
            p.wait("scalar", ch)
            p.dma("scalar", g48_t[0:1, pp * 16:pp * 16 + 16],
                  eg_t[0:16, pp:pp + 1], ch)
        p.wait("scalar", ch)
        p.dma("scalar", tiny_t[0:1, C_POL:C_POL + 3], pl3_t[0:3, 0:1], ch)
        p.wait("vector", ch)
        g3x16 = g48_t[0:1, :].rearrange("p (a b) -> p a b", b=16)
        p.op("vector", "reduce_max", tiny_t[0:1, C_SMX:C_SMX + 3], g3x16, AX.X, inc=ve)
        p.wait("gpsimd", ve)
        for pp in range(3):
            p.op("gpsimd", "tensor_scalar", g48_t[0:1, pp * 16:pp * 16 + 16],
                 g48_t[0:1, pp * 16:pp * 16 + 16],
                 tiny_t[0:1, C_SMX + pp:C_SMX + pp + 1], None, ALU.subtract,
                 inc=ge)
        p.wait("scalar", ve)
        p.wait("scalar", ge)
        p.op("scalar", "activation", g48_t[0:1, :], g48_t[0:1, :], AF.Exp, inc=se)
        p.wait("vector", se)
        p.op("vector", "reduce_sum", tiny_t[0:1, C_GS0:C_GS0 + 3], g3x16, AX.X, inc=ve)
        p.op("vector", "reduce_max", tiny_t[0:1, C_PMX:C_PMX + 1],
             tiny_t[0:1, C_POL:C_POL + 3], AX.X, inc=ve)
        p.wait("gpsimd", ve)
        p.op("gpsimd", "tensor_scalar", tiny_t[0:1, C_POL:C_POL + 3],
             tiny_t[0:1, C_POL:C_POL + 3], tiny_t[0:1, C_PMX:C_PMX + 1],
             None, ALU.subtract, inc=ge)
        p.wait("scalar", ve)
        p.wait("scalar", ge)
        p.op("scalar", "activation", tiny_t[0:1, C_POL:C_POL + 3],
             tiny_t[0:1, C_POL:C_POL + 3], AF.Exp, inc=se)
        p.wait("vector", se)
        p.op("vector", "reduce_sum", tiny_t[0:1, C_PS:C_PS + 1],
             tiny_t[0:1, C_POL:C_POL + 3], AX.X, inc=ve)
        p.wait("gpsimd", ve)
        p.op("gpsimd", "tensor_copy", tiny_t[0:1, C_PS2:C_PS2 + 1],
             tiny_t[0:1, C_PS:C_PS + 1], inc=ge)
        p.wait("vector", ge)
        p.op("vector", "reciprocal", tiny_t[0:1, C_PSR:C_PSR + 4],
             tiny_t[0:1, C_PS2:C_PS2 + 4], inc=ve)
        p.op("vector", "reciprocal", tiny_t[0:1, C_GSR:C_GSR + 4],
             tiny_t[0:1, C_GS0:C_GS0 + 4], inc=ve)
        p.wait("gpsimd", ve)
        p.op("gpsimd", "tensor_scalar", tiny_t[0:1, C_POL:C_POL + 3],
             tiny_t[0:1, C_POL:C_POL + 3], tiny_t[0:1, C_PSR:C_PSR + 1],
             None, ALU.mult, inc=ge)
        p.op("gpsimd", "tensor_mul", tiny_t[0:1, C_POL:C_POL + 3],
             tiny_t[0:1, C_POL:C_POL + 3], tiny_t[0:1, C_GSR:C_GSR + 3], inc=ge)
        p.op("gpsimd", "tensor_scalar", hg_t[0:1, :], g48_t[0:1, 0:16],
             tiny_t[0:1, C_POL:C_POL + 1], None, ALU.mult, inc=ge)
        for pp in (1, 2):
            p.op("gpsimd", "tensor_scalar", g48_t[0:1, pp * 16:pp * 16 + 16],
                 g48_t[0:1, pp * 16:pp * 16 + 16],
                 tiny_t[0:1, C_POL + pp:C_POL + pp + 1], None, ALU.mult, inc=ge)
            p.op("gpsimd", "tensor_add", hg_t[0:1, :], hg_t[0:1, :],
                 g48_t[0:1, pp * 16:pp * 16 + 16], inc=ge)
        p.op("gpsimd", "tensor_scalar", srow_t[0:1, 0:16], hg_t[0:1, :],
             1.0 / 8.0, None, ALU.mult, inc=ge)
        p.wait("scalar", ge)
        bcast(gsc_t[:], srow_t[0:1, 0:16])
        p.wait("vector", ch)
        for oc in range(NCH):
            p.op("vector", "tensor_copy", qsc_t[0:64, oc:oc + 1],
                 gsc_t[0:64, 2 * oc:2 * oc + 1])
            p.op("vector", "tensor_copy", qsc_t[64:128, oc:oc + 1],
                 gsc_t[64:128, 2 * oc + 1:2 * oc + 2], inc=ve)
        p.op("vector", "tensor_mul", bqg_t[:], aqb_t[:], qsc_t[:], inc=ve)
        return ve.n, pretoks

    def attn_core_a(gates_ready):
        half = NCH * T // 2
        if T >= 2048:
            sring = [xln_f32[:, f32b:f32b + 512],
                     xln_f32[:, f32b + 512:f32b + 1024],
                     xln_f32[:, f32b + 1024:f32b + 1536],
                     xln_f32[:, half - 1024:half - 512]]
            rscr = xln_f32[:, half - 512:half]
        else:
            sring = [dvs_t[:, 0:512], dvs_t[:, 512:1024]]
            rscr = dvs_t[:, 1024:1536]
        eslot = 0
        pidx = 0
        dq = []

        def emit_S(qt, pr, kc):
            q0 = qt * 512
            w = TTW
            pS = pSA if kc % 2 == 0 else pSB
            b0 = 4 if kc % 2 == 0 else 6
            banks.use(b0, "tensor")
            banks.use(b0 + 1, "tensor")
            ksl = slice(kc * 128, kc * 128 + 128)
            qsl = slice(q0, q0 + w)
            p.op("tensor", "matmul", pS[:, 0:w],
                 B_t[0:64, pr, ksl], A_t[0:64, pr, qsl],
                 start=True, stop=True, tile_position=(0, 0))
            p.op("tensor", "matmul", pS[:, 512:512 + w],
                 B_t[64:128, pr, ksl], A_t[64:128, pr, qsl],
                 start=True, stop=True, tile_position=(64, 0), inc=te)
            off = kc * 128 - q0
            if off >= 0:
                p.wait("vector", te)
                for base in (0, 512):
                    if off > 0:
                        p.op("vector", "tensor_add", pS[:, base:base + off],
                             pS[:, base:base + off], negf_t[:, 0:off])
                    p.op("vector", "tensor_add",
                         pS[:, base + off:base + off + 128],
                         pS[:, base + off:base + off + 128], mask_t[:],
                         inc=(ve if base == 512 else None))
            return off

        def defer_divide(pr, q0, w, ri):
            sv = se.n
            h = w // 2

            def r(base, lo, hi, first=False, sv=sv, ri=ri):
                def f():
                    if first:
                        p.wait("vector", se, sv)
                    p.op("vector", "reciprocal", rscr[base:base + 64, lo:hi],
                         sring[ri][base:base + 64, lo:hi], inc=ve)
                return f

            def m(base, pr=pr, q0=q0, w=w):
                def f():
                    p.op("vector", "tensor_mul",
                         A_t[base:base + 64, pr, q0:q0 + w],
                         A_t[base:base + 64, pr, q0:q0 + w],
                         rscr[base:base + 64, 0:w], inc=ve)
                return f
            dq.extend([r(0, 0, h, True), r(0, h, w), m(0),
                       r(64, 0, h), r(64, h, w), m(64)])

        for qt in range(NT):
            q0 = qt * 512
            w = TTW
            nkc = (q0 + w) // 128
            for pr in range(NCH):
                for b in range(4):
                    banks.use(b, "tensor")
                off = emit_S(qt, pr, 0)
                offs = {0: off}
                e2s = {}
                for kc in range(nkc):
                    pS = pSA if kc % 2 == 0 else pSB
                    b0 = 4 if kc % 2 == 0 else 6
                    e2 = E2buf(eslot)
                    kE = eslot % (NE // 2)
                    eslot += 1
                    pass
                    esl.use(kE, "scalar")
                    p.wait("scalar", te)
                    if offs[kc] >= 0:
                        p.wait("scalar", ve)
                    p.wait("scalar", ch, gates_ready)
                    p.op("scalar", "activation", e2[:, 0:512 + w],
                         pS[:, 0:512 + w], AF.Exp, bias=nba_t[:], inc=se)
                    e2s[kc] = (e2, kE, se.n)
                    banks.done(b0, se)
                    banks.done(b0 + 1, se)
                    if dq:
                        dq.pop(0)()
                    if kc + 1 < nkc:
                        offs[kc + 1] = emit_S(qt, pr, kc + 1)
                    for kv in ([kc - 1] if kc >= 1 else []) + ([kc] if kc == nkc - 1 else []):
                        ev, kEv, sev = e2s.pop(kv)
                        p.wait("tensor", se, sev)
                        st = (kv == 0)
                        sp = (kv == nkc - 1)
                        p.op("tensor", "matmul", pbk[0][0:64, 0:w],
                             C_t[:, kv, pr * 128:pr * 128 + 64], ev[:, 0:w],
                             start=st, stop=sp, tile_position=(0, 0))
                        p.op("tensor", "matmul", pbk[2][0:64, 0:w],
                             ones64_t[:], ev[:, 0:w],
                             start=st, stop=sp, tile_position=(0, 0))
                        p.op("tensor", "matmul", pbk[1][64:128, 0:w],
                             C_t[:, kv, pr * 128 + 64:pr * 128 + 128],
                             ev[:, 512:512 + w],
                             start=st, stop=sp, tile_position=(0, 64))
                        p.op("tensor", "matmul", pbk[3][64:128, 0:w],
                             ones64_t[:], ev[:, 512:512 + w],
                             start=st, stop=sp, tile_position=(0, 64), inc=te)
                        esl.done(kEv, te)
                while len(dq) > 2:
                    dq.pop(0)()
                ri = pidx % len(sring)
                pidx += 1
                p.wait("scalar", te)
                p.op("scalar", "activation", A_t[0:64, pr, q0:q0 + w],
                     pbk[0][0:64, 0:w], AF.Identity, inc=se)
                p.op("scalar", "activation", A_t[64:128, pr, q0:q0 + w],
                     pbk[1][64:128, 0:w], AF.Identity, inc=se)
                banks.done(0, se)
                banks.done(1, se)
                p.wait("vector", te)
                p.op("vector", "tensor_copy", sring[ri][0:64, 0:w],
                     pbk[2][0:64, 0:w], inc=ve)
                p.op("vector", "tensor_copy", sring[ri][64:128, 0:w],
                     pbk[3][64:128, 0:w], inc=ve)
                banks.done(2, ve)
                banks.done(3, ve)
                defer_divide(pr, q0, w, ri)
        while dq:
            dq.pop(0)()

    def wo_resid(seq):
        p.wait("tensor", ve)
        res = []
        for oc in range(NCH):
            pc, tok = ring.load(ao_h[oc], 1, shape=(8, 128))
            ring.use(tok)
            bset = (oc % 2) * 4
            if seq == 0:
                banks.use(bset, "tensor")
                for kc in range(NCH):
                    p.op("tensor", "matmul", pbk[bset][0:128, 0:1],
                         pc[:, kc, :], avbb_t[:, kc:kc + 1],
                         start=(kc == 0), stop=(kc == NCH - 1),
                         inc=(te if kc == NCH - 1 else None))
                p.wait("scalar", te)
                p.op("scalar", "activation", bopr_t[:, oc:oc + 1],
                     pbk[bset][0:128, 0:1], AF.Identity,
                     bias=aob_t[:, oc:oc + 1], inc=se)
                banks.done(bset, se)
                p.wait("vector", se)
            for b in range(bset, bset + NT):
                banks.use(b, "tensor")
            for kc in range(NCH):
                for tt in range(NT):
                    bank = bset + tt
                    p.op("tensor", "matmul", pbk[bank][:, 0:TTW],
                         pc[:, kc, :], A_t[:, kc, tt * 512:tt * 512 + TTW],
                         start=(kc == 0), stop=(kc == NCH - 1),
                         inc=(te if (kc == NCH - 1 and tt == NT - 1) else None))
            ring.free(tok, te)
            p.wait("vector", te)
            for tt in range(NT):
                bank = bset + tt
                p.op("vector", "scalar_tensor_tensor",
                     xres_t[:, oc, tt * 512:tt * 512 + TTW],
                     pbk[bank][:, 0:TTW], bopr_t[:, oc:oc + 1],
                     xres_t[:, oc, tt * 512:tt * 512 + TTW],
                     op0=ALU.add, op1=ALU.add, inc=ve)
                banks.done(bank, ve)
            res.append((ve, ve.n))
        return res


    def ffn(aff, stop_after_h=False, hook=None):
        nh_tiles = NFH * NT
        assert nh_tiles * 512 <= 3 * NCH * T

        def hsl(hc, tt):
            i = hc * NT + tt
            off = i * 512
            if off < NCH * T:
                return A_f[:, off:off + TTW]
            off -= NCH * T
            if off < NCH * T:
                return B_f[:, off:off + TTW]
            off -= NCH * T
            return C_f[:, off:off + TTW]
        h_ve = []
        for hc in range(NFH):
            gu, gtok = ring.load(fgu_h[hc], 2, shape=(16, 128))
            gp = gu[:, 0:8, :]
            up = gu[:, 8:16, :]
            utok = gtok
            ring.use(gtok)
            for tt in range(NT):
                gb = tt % 2
                ub = 2 + tt % 2
                banks.use(gb, "tensor")
                banks.use(ub, "tensor")
                for kc in range(NCH):
                    if hc == 0 and tt == 0:
                        p.wait("tensor", ve, aff[kc])
                    p.op("tensor", "matmul", pbk[gb][:, 0:TTW], gp[:, kc, :],
                         xln_t[:, kc, tt * 512:tt * 512 + TTW],
                         start=(kc == 0), stop=(kc == NCH - 1),
                         inc=(te if kc == NCH - 1 else None))
                for kc in range(NCH):
                    p.op("tensor", "matmul", pbk[ub][:, 0:TTW], up[:, kc, :],
                         xln_t[:, kc, tt * 512:tt * 512 + TTW],
                         start=(kc == 0), stop=(kc == NCH - 1),
                         inc=(te if kc == NCH - 1 else None))
                p.wait("scalar", te)
                p.op("scalar", "activation", hsl(hc, tt), pbk[gb][:, 0:TTW],
                     AF.Silu, scale=INV127, inc=se)
                banks.done(gb, se)
                p.wait("vector", se)
                p.op("vector", "tensor_mul", hsl(hc, tt), pbk[ub][:, 0:TTW],
                     hsl(hc, tt), inc=ve)
                banks.done(ub, ve)
            ring.free(gtok, te)
            h_ve.append(ve.n)
        if stop_after_h:
            return
        for oc in range(NCH):
            dtoks = []
            for j in range(3):
                dpj, dtok = ring.load(fd_h[oc][:, 8 * j:8 * j + 8, :], 2,
                                      shape=(8, 128))
                dtoks.append((dpj, dtok))
            for _, dtok in dtoks:
                ring.use(dtok)
            for tt in range(NT):
                bank = 4 + tt
                banks.use(bank, "tensor")
                for hc in range(NFH):
                    if oc == 0 and tt == 0:
                        p.wait("tensor", ve, h_ve[hc])
                    dpj = dtoks[hc // 8][0]
                    p.op("tensor", "matmul", pbk[bank][:, 0:TTW],
                         dpj[:, hc % 8, :], hsl(hc, tt),
                         start=(hc == 0), stop=(hc == NFH - 1),
                         inc=(te if hc == NFH - 1 else None))
                p.wait("vector", te)
                p.op("vector", "scalar_tensor_tensor",
                     xres_t[:, oc, tt * 512:tt * 512 + TTW],
                     pbk[bank][:, 0:TTW], INV127 * INV127,
                     xres_t[:, oc, tt * 512:tt * 512 + TTW],
                     op0=ALU.mult, op1=ALU.add, inc=ve)
                banks.done(bank, ve)
            if hook is not None:
                hook(oc)
            for _, dtok in dtoks:
                ring.free(dtok, te)


    def attn_b(aff, hook=None):
        proj_fm(bq_h, bqb_t, A_t, aff, chase=True)
        proj_fm(bk_h, bkb_t, B_t, aff)
        ring.barrier()
        ring.n = 4
        bvb = ring_t[:, 4 * 1024:6 * 1024].bitcast(F32)

        p.dma("scalar", bvb[0:1, :], bvrow_h, ch)
        p.wait("scalar", ch)
        bcast(bvb, bvb[0:1, :])
        p.wait("vector", ch)
        proj_tm(bv_h, aff, bias_bcast=bvb)
        bounds(A_t, C_QB)
        bounds(B_t, C_KB)
        p.wait("vector", se)
        p.op("vector", "tensor_mul", tiny_t[0:1, C_BB:C_BB + 1],
             tiny_t[0:1, C_QB:C_QB + 1], tiny_t[0:1, C_KB:C_KB + 1], inc=ve)
        p.wait("gpsimd", ve)
        p.op("gpsimd", "tensor_scalar", tiny_t[0:1, C_BB:C_BB + 1],
             tiny_t[0:1, C_BB:C_BB + 1], -1.0 / 32.0, None, ALU.mult, inc=ge)
        p.wait("scalar", ve)
        p.wait("scalar", ge)

        bcast(nbb_t[:], tiny_t[0:1, C_BB:C_BB + 1])
        nb_ready = ch.n
        for qt in range(NT):
            q0 = qt * 512
            w = TTW
            nkc = (q0 + w) // 128
            ebufs = []
            for kc in range(nkc):
                sbank = 4 + (kc % 2)
                banks.use(sbank, "tensor")
                ksl = slice(kc * 128, kc * 128 + 128)
                for dc in range(NCH):
                    p.op("tensor", "matmul", pbk[sbank][:, 0:w],
                         B_t[:, dc, ksl], A_t[:, dc, q0:q0 + w],
                         start=(dc == 0), stop=(dc == NCH - 1),
                         inc=(te if dc == NCH - 1 else None))
                off = kc * 128 - q0
                if off >= 0:
                    p.wait("vector", te)
                    p.op("vector", "tensor_add", pbk[sbank][:, off:off + 128],
                         pbk[sbank][:, off:off + 128], mask_t[:], inc=ve)
                eb = Ebuf(kc)
                ebufs.append(eb)
                esl.use(kc % NE, "scalar")
                p.wait("scalar", te)
                if off >= 0:
                    p.wait("scalar", ve)
                p.wait("scalar", ch, nb_ready)
                ecols = slice(max(0, off), w)
                p.op("scalar", "activation", eb[:, ecols], pbk[sbank][:, ecols],
                     AF.Exp, scale=1.0 / 32.0, bias=nbb_t[:], inc=se)
                banks.done(sbank, se)
                if off > 0:
                    esl.use(kc % NE, "vector")
                    p.op("vector", "memset", eb[:, 0:off], 0.0, inc=ve)
            e_se = se.n
            e_ve = ve.n
            p.wait("tensor", se, e_se)
            p.wait("tensor", ve, e_ve)
            banks.use(2, "tensor")
            for kc in range(nkc):
                p.op("tensor", "matmul", pbk[2][0:64, 0:w], ones64_t[:],
                     ebufs[kc][:, 0:w], start=(kc == 0), stop=(kc == nkc - 1),
                     inc=(te if kc == nkc - 1 else None))
            p.wait("scalar", te)
            p.op("scalar", "activation", zrow[0:1, q0:q0 + w],
                 pbk[2][0:1, 0:w], AF.Identity, inc=se)
            banks.done(2, se)
            for dc in range(NCH):
                bank = dc % 2
                banks.use(bank, "tensor")
                for kc in range(nkc):
                    p.op("tensor", "matmul", pbk[bank][:, 0:w],
                         C_t[:, kc, dc * 128:dc * 128 + 128],
                         ebufs[kc][:, 0:w],
                         start=(kc == 0), stop=(kc == nkc - 1),
                         inc=(te if kc == nkc - 1 else None))
                if dc == NCH - 1:
                    for kc in range(nkc):
                        esl.done(kc % NE, te)
                p.wait("scalar", te)
                p.op("scalar", "activation", A_t[:, dc, q0:q0 + w],
                     pbk[bank][:, 0:w], AF.Identity, inc=se)
                banks.done(bank, se)
        p.wait("vector", se)
        p.op("vector", "reduce_sum", tiny_t[0:1, C_Z:C_Z + 1],
             zrow[0:1, 0:T], AX.X, inc=ve)
        p.wait("gpsimd", ve)
        p.op("gpsimd", "tensor_copy", tiny_t[0:1, C_PS2:C_PS2 + 1],
             tiny_t[0:1, C_Z:C_Z + 1], inc=ge)
        p.wait("vector", ge)
        p.op("vector", "reciprocal", tiny_t[0:1, C_ZR:C_ZR + 4],
             tiny_t[0:1, C_PS2:C_PS2 + 4], inc=ve)
        p.wait("scalar", ve)

        bcast(rzb_t[:], tiny_t[0:1, C_ZR:C_ZR + 1])
        p.wait("vector", ch)
        p.wait("vector", se)
        for dc in range(NCH):
            for tt in range(NT):
                p.op("vector", "scalar_tensor_tensor",
                     xres_t[:, dc, tt * 512:tt * 512 + TTW],
                     A_t[:, dc, tt * 512:tt * 512 + TTW], rzb_t[:],
                     xres_t[:, dc, tt * 512:tt * 512 + TTW],
                     op0=ALU.mult, op1=ALU.add, inc=ve)
            if hook is not None:
                hook(dc)


    stores_before = 0
    for seq in range(n_dram_seqs):
        p.wait("sync", ve)
        p.wait("sync", qo, stores_before)
        res1 = []
        for c in range(NCH):
            s = qLs[c % 4]
            p.dma("sync", xres_t[:, c, :], x_h[seq, c], s)
            res1.append((s, s.n))

        p.wait("scalar", te)
        p.wait("scalar", ve)
        st = ln_begin()
        for c in range(NCH):
            ln_chunk(st, c, res1[c])
        xr = ln_finish(st, first=True)
        if dbg_stop == "ln1":
            break
        proj_fm(ak_h, akb_t, B_t, xr, chase=True)
        gq, pre_aq = gating(xr, pre_h=aq_h, npre=2)
        if dbg_stop == "gating":
            break
        p.wait("scalar", ve, gq)
        proj_fm(aq_h, bqg_t, A_t, xr, scale_t=qsc_t, pre=pre_aq)
        proj_tm(av_h, xr)
        if dbg_stop == "proj":
            break
        bounds(A_t, C_QA)
        bounds(B_t, C_KA)
        if dbg_stop == "bounds":
            break
        p.wait("vector", se)
        p.op("vector", "tensor_mul", tiny_t[0:1, C_BA:C_BA + 1],
             tiny_t[0:1, C_QA:C_QA + 1], tiny_t[0:1, C_KA:C_KA + 1], inc=ve)
        p.wait("gpsimd", ve)
        p.op("gpsimd", "tensor_scalar", tiny_t[0:1, C_NB8:C_NB8 + 1],
             tiny_t[0:1, C_BA:C_BA + 1], -1.0, None, ALU.mult, inc=ge)
        p.wait("scalar", ge)
        bcast(nba_t[:], tiny_t[0:1, C_NB8:C_NB8 + 1])
        gr = ch.n
        attn_core_a(gr)
        if dbg_stop == "attn_core":
            break
        res = wo_resid(seq)
        ring.barrier()
        if dbg_stop == "attn_a":
            break

        st = ln_begin(scr=lambda c: B_t[:, 4 + c % 4, :])
        for c in range(NCH):
            ln_chunk(st, c, res[c])
        xr = ln_finish(st)
        if dbg_stop == "ln2":
            break
        ffn(xr, stop_after_h=(dbg_stop == "ffn_h"))
        if dbg_stop == "ffn_h":
            break
        ring.barrier()
        if dbg_stop == "ffn1":
            break
        p.wait("scalar", te)
        p.wait("scalar", ve)
        st = ln_begin()
        for c in range(NCH):
            ln_chunk(st, c)
        xr = ln_finish(st)

        st = ln_begin(scr=lambda c: A_t[:, c, :], guard=False)
        attn_b(xr, hook=(lambda dc, s=st: ln_chunk(s, dc, (ve, ve.n))))
        ring.freed[4].append((ve, ve.n))
        ring.freed[5].append((ve, ve.n))
        ring.n = ring.nmax
        ring.barrier()
        if dbg_stop == "attn_b":
            break
        xr = ln_finish(st)

        ffn(xr)
        ring.barrier()

        p.wait("sync", ve)
        for c in range(NCH):
            p.dma("sync", out_h[seq, c], xres_t[:, c, :], qo)
        stores_before = qo.n

    if dbg_stop:
        p.wait("sync", ve)
        p.wait("sync", se)
        p.wait("sync", te)
        p.wait("sync", ch)
        p.dma("sync", dbgA_h, A_t[:], qo)
        p.dma("sync", dbgB_h, B_t[:], qo)
        p.dma("sync", dbgC_h, C_t[:], qo)
        p.dma("sync", dbgX_h, xln_t[:], qo)
        p.dma("sync", dbgR_h, xres_t[:], qo)
        p.dma("sync", dbgS_h[:, 0:16], gsc_t[:], qo)
        p.dma("sync", dbgS_h[:, 16:32], gsc_t[:], qo)
        p.dma("sync", dbgS_h[0:4, 32:96], tiny_t[:], qo)
        p.dma("sync", dbgS_h[0:1, 96:112], hg_t[:], qo)
        p.dma("sync", dbgS_h[0:1, 112:160], g48_t[:], qo)
        stores_before = qo.n
    p.wait("gpsimd", qo, stores_before)
    p.op("gpsimd", "engine_nop")

    with nc.Block() as block:
        p.build(block)
    return nc



def _prep_weights(inputs):
    f = np.float32
    g = {}

    def piece_oc(wT):
        return np.ascontiguousarray(
            wT.reshape(8, 128, 8, 128).transpose(2, 1, 0, 3))

    def piece_col(wT):
        return np.ascontiguousarray(
            wT.reshape(8, 128, 2, 512).transpose(2, 1, 0, 3))

    def pm(b):
        return np.ascontiguousarray(np.asarray(b, f).reshape(8, 128).T)

    wq = np.asarray(inputs["L0_attn_a_wq"], f)
    wk = np.asarray(inputs["L0_attn_a_wk"], f)
    wv = np.asarray(inputs["L0_attn_a_wv"], f)
    wo = np.asarray(inputs["L0_attn_a_wo"], f)
    g["aq"] = piece_oc(np.ascontiguousarray(wq.T))
    g["ak"] = piece_oc(np.ascontiguousarray(wk.T))
    g["ao"] = piece_oc(np.ascontiguousarray(wo.T))
    g["av"] = piece_col(np.ascontiguousarray(wv.T))
    g["aqb"] = pm(inputs["L0_attn_a_bq"])
    g["akb"] = pm(inputs["L0_attn_a_bk"])
    g["avb"] = pm(inputs["L0_attn_a_bv"])
    g["aob"] = pm(inputs["L0_attn_a_bo"])
    pw1 = np.asarray(inputs["L0_attn_a_pw1"], f)
    g["pw1t"] = np.ascontiguousarray(
        pw1.transpose(0, 2, 1).reshape(3, 8, 128, 512).transpose(0, 2, 1, 3))
    pb1 = np.asarray(inputs["L0_attn_a_pb1"], f)
    g["pb1"] = np.ascontiguousarray(
        pb1.reshape(3, 4, 128).transpose(2, 0, 1).reshape(128, 12))
    pw2 = np.asarray(inputs["L0_attn_a_pw2"], f)
    g["pw2t"] = np.ascontiguousarray(
        pw2.transpose(0, 2, 1).reshape(3, 4, 128, 16)
        .transpose(2, 0, 1, 3).reshape(128, 192))
    g["pb2t"] = np.ascontiguousarray(np.asarray(inputs["L0_attn_a_pb2"], f).T)
    sw1 = np.asarray(inputs["L0_attn_a_sw1"], f)
    g["sw1t"] = np.ascontiguousarray(
        sw1.T.reshape(8, 128, 512).transpose(1, 0, 2))
    g["sb1"] = np.ascontiguousarray(
        np.asarray(inputs["L0_attn_a_sb1"], f).reshape(4, 128).T)
    sw2 = np.asarray(inputs["L0_attn_a_sw2"], f)
    g["sw2t"] = np.ascontiguousarray(
        sw2.T.reshape(4, 128, 3).transpose(1, 0, 2).reshape(128, 12))
    g["sb2"] = np.ascontiguousarray(
        np.asarray(inputs["L0_attn_a_sb2"], f).reshape(3, 1))
    Wq = np.asarray(inputs["L0_attn_b_Wq"], f)
    Wk = np.asarray(inputs["L0_attn_b_Wk"], f)
    Wv = np.asarray(inputs["L0_attn_b_Wv"], f)
    g["bq"] = piece_oc(Wq)
    g["bk"] = piece_oc(Wk)
    g["bv"] = piece_col(Wv)
    g["bqb"] = pm(inputs["L0_attn_b_bq"])
    g["bkb"] = pm(inputs["L0_attn_b_bk"])
    g["bvrow"] = np.ascontiguousarray(
        np.asarray(inputs["L0_attn_b_bv"], f).reshape(1, 1024))
    wg = np.asarray(inputs["L0_ffn_w_gate"], np.int32)
    wu = np.asarray(inputs["L0_ffn_w_up"], np.int32)
    wd = np.asarray(inputs["L0_ffn_w_down"], np.int32)
    fg = wg.T.reshape(8, 128, 24, 128).transpose(2, 1, 0, 3)
    fu = wu.T.reshape(8, 128, 24, 128).transpose(2, 1, 0, 3)
    g["fgu"] = np.ascontiguousarray(np.concatenate([fg, fu], axis=2))
    g["fd"] = np.ascontiguousarray(
        wd.T.reshape(24, 128, 8, 128).transpose(2, 1, 0, 3))
    g["mask"] = np.where(np.arange(128)[:, None] > np.arange(128)[None, :],
                         -1e30, 0.0).astype(BF)
    cs = np.zeros((128, 8, 8), BF)
    for j in range(8):
        cs[:, j, j] = 1.0
    g["colsel"] = cs
    return g


def kernel(**inputs):
    T = 2048
    n_cores = 8
    spc = 2
    x = np.asarray(inputs["L0_attn_a_x"], np.float32)
    g = _prep_weights(inputs)
    nc = build_program(T=T, n_dram_seqs=spc)
    in_maps = []
    for c in range(n_cores):
        m = dict(g)
        xs = np.empty((spc, NCH, 128, T), np.float32)
        for s in range(spc):
            xs[s] = x[c * spc + s].T.reshape(NCH, 128, T)
        m["x"] = xs
        in_maps.append(m)
    res = run_bass_kernel_spmd(nc, in_maps, core_ids=list(range(n_cores)))
    out = np.empty_like(x)
    for c in range(n_cores):
        o = np.asarray(res.results[c]["out"])
        for s in range(spc):
            out[c * spc + s] = o[s].reshape(D, T).T
    return out


# version 13
# speedup vs baseline: 1.5585x; 1.0196x over previous
import numpy as np
import ml_dtypes
import concourse.bass as bass
import concourse.mybir as mybir
from concourse.bass_utils import run_bass_kernel_spmd

F32 = mybir.dt.float32
BF16 = mybir.dt.bfloat16
I32 = mybir.dt.int32
AF = mybir.ActivationFunctionType
ALU = mybir.AluOpType
AX = mybir.AxisListType
BF = ml_dtypes.bfloat16

ENGINES = ("sync", "vector", "scalar", "tensor", "gpsimd")
D = 1024
NCH = 8
NFH = 24
EPS = 1e-5
INV127 = float(np.float32(1.0 / 127.0))


class Sem:
    __slots__ = ("h", "n", "name")

    def __init__(self, handle, name=""):
        self.h = handle
        self.n = 0
        self.name = name


class Prog:
    def __init__(self, nc):
        self.nc = nc
        self.streams = {e: [] for e in ENGINES}
        self.nins = 0

    def sem(self, name):
        return Sem(self.nc.semaphore(name).__enter__(), name)

    def emit(self, engine, fn):
        self.streams[engine].append(fn)

    def wait(self, engine, sem, val=None):
        v = sem.n if val is None else val
        if v <= 0:
            return
        self.emit(engine, lambda eng, s=sem.h, v=v: eng.wait_ge(s, v))

    def op(self, engine, method, *args, inc=None, **kw):
        self.nins += 1
        if inc is not None:
            sem, k = inc if isinstance(inc, tuple) else (inc, 1)
            sem.n += k

            def fn(eng, m=method, a=args, kw=kw, s=sem.h, k=k):
                getattr(eng, m)(*a, **kw).then_inc(s, k)
        else:
            def fn(eng, m=method, a=args, kw=kw):
                getattr(eng, m)(*a, **kw)
        self.emit(engine, fn)

    def dma(self, engine, out, in_, sem):
        self.op(engine, "dma_start", inc=(sem, 16), out=out, in_=in_)

    def build(self, block):
        for e in ENGINES:
            stream = self.streams[e]
            if not stream:
                continue

            def runner(eng, stream=stream):
                for fn in stream:
                    fn(eng)
            getattr(block, e)(runner)


class Banks:

    def __init__(self, p):
        self.p = p
        self.pend = {i: [] for i in range(8)}

    def use(self, i, engine):
        for sem, val in self.pend[i]:
            self.p.wait(engine, sem, val)
        self.pend[i] = []

    def done(self, i, sem):
        self.pend[i].append((sem, sem.n))


class Slots:

    def __init__(self, p):
        self.p = p
        self.pend = {}

    def use(self, key, engine):
        for sem, val in self.pend.get(key, ()):
            self.p.wait(engine, sem, val)
        self.pend[key] = []

    def done(self, key, sem):
        self.pend.setdefault(key, []).append((sem, sem.n))


class Ring:

    def __init__(self, p, area_bf, nslots=8):
        self.p = p
        self.area = area_bf
        self.n = nslots
        self.nmax = nslots
        self.sems = [p.sem(f"wr{i}") for i in range(nslots)]
        self.freed = {i: [] for i in range(nslots)}
        self.next = 0

    def load(self, dram_ap, width_slots=1, shape=None, cols=None):
        if self.next + width_slots > self.n:
            self.next = 0
        i0 = self.next
        self.next += width_slots
        slots = list(range(i0, i0 + width_slots))
        for i in slots:
            for sem, val in self.freed[i]:
                self.p.wait("gpsimd", sem, val)
            self.freed[i] = []
        if cols is None and shape is not None:
            n = 1
            for d in shape:
                n *= d
            cols = n
        ncols = cols if cols is not None else width_slots * 1024
        flat = self.area[:, i0 * 1024:i0 * 1024 + ncols]
        view = flat
        if shape is not None:
            view = flat.rearrange("p (a b) -> p a b", b=shape[-1])
        sem = self.sems[i0]
        self.p.dma("gpsimd", flat[0:64], dram_ap[0:64], sem)
        self.p.dma("gpsimd", flat[64:128], dram_ap[64:128], sem)
        return view, (slots, sem, sem.n)

    def use(self, token, engine="tensor"):
        _, sem, val = token
        self.p.wait(engine, sem, val)

    def free(self, token, sem):
        slots, _, _ = token
        for i in slots:
            self.freed[i].append((sem, sem.n))

    def barrier(self):
        for i in range(self.nmax):
            for sem, val in self.freed[i]:
                self.p.wait("gpsimd", sem, val)
            self.freed[i] = []
        self.next = 0


def build_program(T=2048, n_dram_seqs=2, dbg_stop=None):
    NT = max(1, T // 512)
    TTW = min(512, T)
    NKC = T // 128
    NE = min(16, NKC)
    nc = bass.Bass("TRN2", target_bir_lowering=False, debug=False,
                   detect_race_conditions=False, dynamic_dma_scratch_size=16384)

    def din(name, shape, dtype=F32):
        return nc.dram_tensor(name, list(shape), dtype, kind="ExternalInput").ap()

    x_h = din("x", [n_dram_seqs, NCH, 128, T])
    out_h = nc.dram_tensor("out", [n_dram_seqs, NCH, 128, T], F32,
                           kind="ExternalOutput").ap()
    aq_h = din("aq", [8, 128, 8, 128])
    ak_h = din("ak", [8, 128, 8, 128])
    ao_h = din("ao", [8, 128, 8, 128])
    av_h = din("av", [2, 128, 8, 512])
    aqb_h = din("aqb", [128, 8])
    akb_h = din("akb", [128, 8])
    avb_h = din("avb", [128, 8])
    aob_h = din("aob", [128, 8])
    pw1_h = din("pw1t", [3, 128, 8, 512])
    pb1_h = din("pb1", [128, 12])
    pw2_h = din("pw2t", [128, 192])
    pb2_h = din("pb2t", [16, 3])
    sw1_h = din("sw1t", [128, 8, 512])
    sb1_h = din("sb1", [128, 4])
    sw2_h = din("sw2t", [128, 12])
    sb2_h = din("sb2", [3, 1])
    bq_h = din("bq", [8, 128, 8, 128])
    bk_h = din("bk", [8, 128, 8, 128])
    bv_h = din("bv", [2, 128, 8, 512])
    bqb_h = din("bqb", [128, 8])
    bkb_h = din("bkb", [128, 8])
    bvrow_h = din("bvrow", [1, 1024])
    fgu_h = din("fgu", [NFH, 128, 16, 128], I32)
    fd_h = din("fd", [8, 128, NFH, 128], I32)
    mask_h = din("mask", [128, 128], BF16)
    colsel_h = din("colsel", [128, 8, 8], BF16)
    if dbg_stop:
        dbgA_h = nc.dram_tensor("dbgA", [128, NCH, T], BF16, kind="ExternalOutput").ap()
        dbgB_h = nc.dram_tensor("dbgB", [128, NCH, T], BF16, kind="ExternalOutput").ap()
        dbgC_h = nc.dram_tensor("dbgC", [128, NKC, 1024], BF16, kind="ExternalOutput").ap()
        dbgX_h = nc.dram_tensor("dbgX", [128, NCH, T], BF16, kind="ExternalOutput").ap()
        dbgR_h = nc.dram_tensor("dbgR", [128, NCH, T], F32, kind="ExternalOutput").ap()
        dbgS_h = nc.dram_tensor("dbgS", [128, 192], F32, kind="ExternalOutput").ap()
        dbgN_h = nc.dram_tensor("dbgN", [4, 512], F32, kind="ExternalOutput").ap()
        dbgQ_h = nc.dram_tensor("dbgQ", [128, 2048], BF16, kind="ExternalOutput").ap()

    p = Prog(nc)
    banks = Banks(p)
    esl = Slots(p)

    te = p.sem("te")
    ve = p.sem("ve")
    se = p.sem("se")
    ge = p.sem("ge")
    qx = p.sem("qx")
    qo = p.sem("qo")
    ch = p.sem("ch")
    qLs = [p.sem(f"qL{i}") for i in range(4)]

    sb_keep = []

    def alloc(name, shape, dtype):
        t = nc.sbuf_tensor(name, list(shape), dtype).__enter__()
        sb_keep.append(t)
        return t

    xres_t = alloc("xres", [128, NCH, T], F32)
    xln_t = alloc("xln", [128, NCH, T], BF16)
    A_t = alloc("Abuf", [128, NCH, T], BF16)
    B_t = alloc("Bbuf", [128, NCH, T], BF16)
    C_t = alloc("Cbuf", [128, NKC, 1024], BF16)
    ring_t = alloc("wring", [128, 6 * 1024], BF16)
    mask_t = alloc("mask_s", [128, 128], BF16)
    colsel_t = alloc("colsel_s", [128, 8, 8], BF16)
    ones64_t = alloc("ones64", [128, 64], BF16)
    aqb_t = alloc("aqb_s", [128, 8], F32)
    akb_t = alloc("akb_s", [128, 8], F32)
    avb_t = alloc("avb_s", [128, 8], F32)
    aob_t = alloc("aob_s", [128, 8], F32)
    bopr_t = alloc("bopr", [128, 8], F32)
    avbb_t = alloc("avbb", [128, 8], BF16)
    bqb_t = alloc("bqb_s", [128, 8], F32)
    bkb_t = alloc("bkb_s", [128, 8], F32)
    pb1_t = alloc("pb1_s", [128, 12], F32)
    pb2_t = alloc("pb2_s", [16, 3], F32)
    sb1_t = alloc("sb1_s", [128, 4], F32)
    sb2_t = alloc("sb2_s", [3, 1], F32)
    xm_t = alloc("xm", [128, 8], F32)
    xmb_t = alloc("xmb", [128, 8], BF16)
    hid_t = alloc("hid", [128, 12], F32)
    hidb_t = alloc("hidb", [128, 12], BF16)
    phid_t = alloc("phid", [128, 4], F32)
    phidb_t = alloc("phidb", [128, 4], BF16)
    eg_t = alloc("eg", [16, 3], F32)
    pl3_t = alloc("pl3", [3, 1], F32)
    g48_t = alloc("g48", [1, 48], F32)
    hg_t = alloc("hg", [1, 16], F32)
    srow_t = alloc("srow", [1, 32], F32)
    gsc_t = alloc("gsc", [128, 16], F32)
    gbi_t = gsc_t
    qsc_t = alloc("qsc", [128, 8], F32)
    bqg_t = alloc("bqg", [128, 8], F32)
    nba_t = alloc("nba", [128, 1], F32)
    negf_t = alloc("negf", [128, 384], BF16)
    tiny_t = alloc("tiny", [4, 64], F32)
    nbb_t = alloc("nbb", [128, 1], F32)
    rzb_t = alloc("rzb", [128, 1], F32)
    silu_t = alloc("silu", [128, 512], BF16)
    selr_t = silu_t
    dvs_t = alloc("dvs", [128, 1536], F32) if T < 2048 else None

    xln_f = xln_t[:].rearrange("p c t -> p (c t)")
    xln_f32 = xln_f.bitcast(F32)
    A_f = A_t[:].rearrange("p c t -> p (c t)")
    B_f = B_t[:].rearrange("p c t -> p (c t)")
    B_f32 = B_f.bitcast(F32)
    C_f = C_t[:].rearrange("p c t -> p (c t)")

    R_bc = B_f[:, 0:T]
    MR_bc = B_f[:, T:2 * T]
    tpass = B_f[:, 2 * T:3 * T]
    rrow_bf = B_f[0:4, 3 * T:3 * T + 512]
    mrrow_bf = B_f[0:4, 3 * T + 512:3 * T + 1024]
    A_f32 = A_f.bitcast(F32)
    lnm = A_f32[0:4, T:T + 512]
    lns2 = A_f32[0:4, T + 512:T + 1024]
    lns3 = A_f32[0:4, T + 1024:T + 1536]


    def Ebuf(i):
        i = i % NE
        return xln_f[:, i * 512:i * 512 + 512]

    def E2buf(i):
        i = (2 * i) % NE
        return xln_f[:, i * 512:i * 512 + 1024]
    sq_base = NE * 512
    sqscr = xln_f[:, sq_base:sq_base + T]
    f32b = sq_base // 2
    ssbA = xln_f32[:, f32b:f32b + 512]
    ssbB = xln_f32[:, f32b + 512:f32b + 1024]
    if f32b + 1024 + T <= NCH * T // 2:
        zrow = xln_f32[0:1, f32b + 1024:f32b + 1024 + T]
    else:
        zrow = xln_f32[0:1, f32b:f32b + T]
    assert f32b + 1024 <= NCH * T // 2
    assert sq_base + T <= NCH * T

    ring = Ring(p, ring_t[:], nslots=6)
    pb03 = [nc.psum_tensor(f"psb{i}", [128, 512], F32).__enter__() for i in range(4)]
    pSA_t = nc.psum_tensor("psA", [128, 1024], F32).__enter__()
    pSB_t = nc.psum_tensor("psB", [128, 1024], F32).__enter__()
    pSA = pSA_t[:]
    pSB = pSB_t[:]
    pbk = [t[:] for t in pb03] + [pSA[:, 0:512], pSA[:, 512:1024],
                                  pSB[:, 0:512], pSB[:, 512:1024]]

    C_GS0 = 0
    C_GSR = 4
    C_POL = 8
    C_PMX = 12
    C_PS = 16
    C_PSR = 20
    C_SMX = 24
    C_NB8 = 28
    C_Z = 32
    C_ZR = 36
    C_QA = 40
    C_KA = 41
    C_BA = 42
    C_QB = 44
    C_KB = 45
    C_BB = 46
    C_G = 48
    C_GM = 52
    C_BT = 56
    C_PS2 = 60

    for dst, src in [(mask_t[:], mask_h), (colsel_t[:], colsel_h),
                     (aqb_t[:], aqb_h), (akb_t[:], akb_h), (avb_t[:], avb_h),
                     (aob_t[:], aob_h), (bqb_t[:], bqb_h), (bkb_t[:], bkb_h),
                     (pb1_t[:], pb1_h), (pb2_t[:], pb2_h), (sb1_t[:], sb1_h),
                     (sb2_t[:], sb2_h)]:
        p.dma("sync", dst, src, qx)
    smalls_ld = qx.n
    p.op("vector", "memset", ones64_t[:], 1.0, inc=ve)
    p.op("vector", "memset", negf_t[:], -1e30, inc=ve)
    p.op("vector", "memset", tiny_t[:], 1.0)
    p.op("vector", "memset", g48_t[:], 0.0)
    p.op("vector", "memset", hg_t[:], 0.0)
    p.op("vector", "memset", srow_t[:], 0.0)
    p.op("vector", "memset", selr_t[:], 0.0, inc=ve)
    p.op("vector", "memset", B_f[:, 3 * T:3 * T + 1024], 0.0, inc=ve)
    p.wait("scalar", ve)
    for j in range(4):
        for h in range(2):
            p.dma("scalar",
                  selr_t[j:j + 1, j * 128 + h * 64:j * 128 + h * 64 + 64],
                  ones64_t[0:1, :], ch)
    selr_ch = ch.n
    p.wait("vector", qx, smalls_ld)
    p.op("vector", "tensor_copy", avbb_t[:], avb_t[:], inc=ve)
    p.wait("tensor", qx, smalls_ld)
    p.wait("vector", qx, smalls_ld)
    p.wait("scalar", qx, smalls_ld)


    def rep_ap(src_row, P):
        from concourse.ap import AP as _AP
        a = src_row.ap
        assert a[0][1] == 1
        return _AP(src_row.tensor, src_row.offset,
                   [[a[0][0], 1], [0, P]] + [list(x) for x in a[1:]])

    def bcast(dst, src_row, queue="scalar", sem=None):
        s = sem or ch
        width = 1
        for d in src_row.shape[1:]:
            width *= d
        if width <= 64:
            p.dma(queue, dst, rep_ap(src_row, dst.shape[0]), s)
            return
        p.dma(queue, dst[0:16], rep_ap(src_row, 16), s)
        k = 16
        while k < dst.shape[0]:
            p.wait(queue, s)
            p.dma(queue, dst[k:2 * k], dst[0:k], s)
            k *= 2

    class LNState:
        def __init__(self, scr, guard, depth):
            self.sq_se = []
            self.stats_te = []
            self.scr = scr
            self.guard = guard
            self.depth = depth

    def ln_begin(scr=None, guard=True, depth=4):
        if scr is None:
            scr = lambda c: A_t[:, c % 4, :]
        return LNState(scr, guard, depth)

    def _ln_stats(st, c):
        if c == 0:
            banks.use(0, "tensor")
            banks.use(1, "tensor")
        sq = st.scr(c)
        p.wait("tensor", se, st.sq_se[c])
        for j in range(NT):
            sl = slice(j * 512, j * 512 + 512)
            p.op("tensor", "matmul", pbk[0][0:4, :], colsel_t[:, j, 0:4],
                 xln_t[:, c, sl], start=(c == 0 and j == 0),
                 stop=(c == NCH - 1 and j == NT - 1))
            p.op("tensor", "matmul", pbk[1][0:4, :], colsel_t[:, j, 0:4],
                 sq[:, sl], start=(c == 0 and j == 0),
                 stop=(c == NCH - 1 and j == NT - 1),
                 inc=(te if j == NT - 1 else None))
        st.stats_te.append(te.n)

    def ln_chunk(st, c, res=None):
        sq = st.scr(c)
        if res is not None:
            p.wait("scalar", res[0], res[1])
        if st.guard and c >= st.depth:
            p.wait("scalar", te, st.stats_te[c - st.depth])
        p.op("scalar", "activation", xln_t[:, c, :], xres_t[:, c, :],
             AF.Identity, inc=se)
        p.op("scalar", "activation", sq, xres_t[:, c, :], AF.Square, inc=se)
        st.sq_se.append(se.n)
        if c >= 1:
            _ln_stats(st, c - 1)

    def ln_finish(st, first=False):
        _ln_stats(st, NCH - 1)
        stx = pbk[0]
        sxx = pbk[1]
        p.wait("vector", te)
        p.op("vector", "tensor_scalar", lnm[0:NT, :], stx[0:NT, :],
             1.0 / D, None, ALU.mult, inc=ve)
        banks.done(0, ve)
        p.op("vector", "tensor_scalar", lns3[0:NT, :], sxx[0:NT, :],
             1.0 / D, EPS, ALU.mult, ALU.add, inc=ve)
        banks.done(1, ve)
        p.wait("gpsimd", ve)
        p.op("gpsimd", "tensor_mul", lns2[0:NT, :], lnm[0:NT, :], lnm[0:NT, :], inc=ge)
        p.op("gpsimd", "tensor_sub", lns3[0:NT, :], lns3[0:NT, :], lns2[0:NT, :], inc=ge)
        p.wait("vector", ge)
        p.op("vector", "reciprocal", lns2[0:NT, :], lns3[0:NT, :], inc=ve)
        p.wait("scalar", ve)
        p.op("scalar", "activation", lns3[0:NT, :], lns2[0:NT, :], AF.Sqrt, inc=se)
        p.wait("vector", se)
        p.op("vector", "tensor_mul", lns2[0:NT, :], lnm[0:NT, :], lns3[0:NT, :], inc=ve)
        p.op("scalar", "activation", rrow_bf[0:NT, :], lns3[0:NT, :],
             AF.Identity, inc=se)
        p.wait("vector", se)
        p.op("vector", "tensor_copy", mrrow_bf[0:NT, :], lns2[0:NT, :], inc=ve)
        p.wait("tensor", ch, selr_ch)
        p.wait("tensor", se)
        p.wait("tensor", ve)
        for j in range(NT):
            rb = 2 + j % 2
            mb = j % 2
            banks.use(rb, "tensor")
            p.op("tensor", "matmul", pbk[rb][:, 0:512],
                 selr_t[:, j * 128:j * 128 + 128],
                 B_f[:, 3 * T:3 * T + 512],
                 start=True, stop=True, inc=te)
            banks.use(mb, "tensor")
            p.op("tensor", "matmul", pbk[mb][:, 0:512],
                 selr_t[:, j * 128:j * 128 + 128],
                 B_f[:, 3 * T + 512:3 * T + 1024],
                 start=True, stop=True, inc=te)
            p.wait("scalar", te)
            p.op("scalar", "activation", R_bc[:, j * 512:j * 512 + 512],
                 pbk[rb][:, 0:512], AF.Identity, inc=se)
            banks.done(rb, se)
            p.wait("vector", te)
            p.op("vector", "tensor_copy", MR_bc[:, j * 512:j * 512 + 512],
                 pbk[mb][:, 0:512], inc=ve)
            banks.done(mb, ve)
        p.wait("vector", se)
        aff = []
        for c in range(NCH):
            p.op("vector", "tensor_mul", tpass, xln_t[:, c, :], R_bc, inc=ve)
            if first:
                p.op("vector", "scalar_tensor_tensor", xln_t[:, c, :], tpass,
                     1.0, MR_bc, op0=ALU.mult, op1=ALU.subtract, inc=ve,
                     accum_out=xm_t[:, c:c + 1])
            else:
                p.op("vector", "tensor_sub", xln_t[:, c, :], tpass, MR_bc,
                     inc=ve)
            aff.append(ve.n)
        return aff

    def proj_fm(w_pieces_h, bias_t, dst_t, aff, scale_t=None, chase=False,
                pre=None):
        if not chase:
            p.wait("tensor", ve, aff[-1])
        for oc in range(NCH):
            if pre is not None and oc < len(pre):
                pc, tok = pre[oc]
            else:
                pc, tok = ring.load(w_pieces_h[oc], 1, shape=(8, 128))
            ring.use(tok)
            bset = (oc % 2) * 4
            for b in range(bset, bset + NT):
                banks.use(b, "tensor")
            for kc in range(NCH):
                if chase and oc == 0:
                    p.wait("tensor", ve, aff[kc])
                for tt in range(NT):
                    bank = bset + tt
                    p.op("tensor", "matmul", pbk[bank][:, 0:TTW],
                         pc[:, kc, :], xln_t[:, kc, tt * 512:tt * 512 + TTW],
                         start=(kc == 0), stop=(kc == NCH - 1),
                         inc=(te if (kc == NCH - 1 and tt == NT - 1) else None))
            ring.free(tok, te)
            p.wait("scalar", te)
            for tt in range(NT):
                bank = bset + tt
                kw = {}
                if scale_t is not None:
                    kw["scale"] = scale_t[:, oc:oc + 1]
                p.op("scalar", "activation",
                     dst_t[:, oc, tt * 512:tt * 512 + TTW],
                     pbk[bank][:, 0:TTW], AF.Identity,
                     bias=bias_t[:, oc:oc + 1], inc=se, **kw)
                banks.done(bank, se)

    def proj_tm(w_col_h, aff, bias_bcast=None):
        p.wait("tensor", ve, aff[-1])
        for colh in range(2):
            pc, tok = ring.load(w_col_h[colh], 4, shape=(8, 512))
            ring.use(tok)
            for tc in range(NKC):
                bank = tc % 4
                banks.use(bank, "tensor")
                for kc in range(NCH):
                    p.op("tensor", "matmul", pbk[bank][:],
                         xln_t[:, kc, tc * 128:tc * 128 + 128], pc[:, kc, :],
                         start=(kc == 0), stop=(kc == NCH - 1),
                         inc=(te if kc == NCH - 1 else None))
                dst = C_t[:, tc, colh * 512:colh * 512 + 512]
                if bias_bcast is None:
                    p.wait("scalar", te)
                    p.op("scalar", "activation", dst, pbk[bank][:], AF.Identity, inc=se)
                    banks.done(bank, se)
                else:
                    p.wait("vector", te)
                    p.op("vector", "tensor_add", dst, pbk[bank][:],
                         bias_bcast[:, colh * 512:colh * 512 + 512], inc=ve)
                    banks.done(bank, ve)
            ring.free(tok, te)

    def bounds(src_t, dst_col):
        bank = 6
        banks.use(bank, "tensor")
        st = pbk[bank]
        p.wait("scalar", te)
        bscr = [sqscr, xln_f[:, 0:T], xln_f[:, T:2 * T], xln_f[:, 2 * T:3 * T]]
        sq_te = []
        for c in range(NCH):
            sq = bscr[c % 4]
            if c >= 4:
                p.wait("scalar", te, sq_te[c - 4])
            p.op("scalar", "activation", sq,
                 src_t[:, c, :], AF.Square, inc=se)
            p.wait("tensor", se)
            for j in range(NT):
                p.op("tensor", "matmul", st[0:8, :], colsel_t[:, j, :],
                     sq[:, j * 512:j * 512 + 512],
                     start=(c == 0 and j == 0),
                     stop=(c == NCH - 1 and j == NT - 1),
                     inc=(te if j == NT - 1 else None))
            sq_te.append(te.n)
        p.wait("vector", te)
        if dbg_stop == "bounds":
            dstage = xln_f32[:, 1280:1792]
            p.op("vector", "tensor_copy", dstage[0:4, :], st[0:4, :], inc=ve)
            p.wait("sync", ve)
            p.dma("sync", dbgN_h, dstage[0:4, :], qo)
            p.dma("sync", dbgQ_h[:, 0:T], sqscr, qo)
        p.op("vector", "reduce_max", tiny_t[0:NT, C_BT:C_BT + 1], st[0:NT, :], AX.X, inc=ve)
        banks.done(bank, ve)
        p.wait("scalar", ve)
        p.wait("scalar", ch)
        p.dma("scalar", tiny_t[0:1, C_G:C_G + NT], tiny_t[0:NT, C_BT:C_BT + 1], ch)
        p.wait("vector", ch)
        p.op("vector", "reduce_max", tiny_t[0:1, C_GM:C_GM + 1],
             tiny_t[0:1, C_G:C_G + NT], AX.X, inc=ve)
        p.wait("scalar", ve)
        p.op("scalar", "activation", tiny_t[0:1, dst_col:dst_col + 1],
             tiny_t[0:1, C_GM:C_GM + 1], AF.Sqrt, inc=se)


    def gating(aff, pre_h=None, npre=0):
        p.wait("vector", ve, aff[-1])
        p.op("vector", "tensor_scalar", xmb_t[:], xm_t[:], 1.0 / T, None,
             ALU.mult, inc=ve)
        xmb_ready = ve.n
        banks.use(5, "tensor")
        hidp = pbk[5]
        p.wait("tensor", ve, xmb_ready)
        for pp in range(3):
            pc, tok = ring.load(pw1_h[pp], 4, shape=(8, 512))
            ring.use(tok)
            for mt in range(4):
                col = pp * 4 + mt
                for kc in range(NCH):
                    last = (pp == 2 and mt == 3 and kc == NCH - 1)
                    p.op("tensor", "matmul", hidp[0:128, col:col + 1],
                         pc[:, kc, mt * 128:mt * 128 + 128],
                         xmb_t[:, kc:kc + 1],
                         start=(kc == 0 and mt == 0 and pp == 0), stop=last,
                         inc=(te if last else None))
            ring.free(tok, te)
        p.wait("vector", te)
        p.op("vector", "tensor_add", hid_t[:], hidp[0:128, 0:12], pb1_t[:], inc=ve)
        banks.done(5, ve)
        p.wait("scalar", ve)
        p.op("scalar", "activation", hidb_t[:], hid_t[:], AF.Gelu, inc=se)
        banks.use(5, "tensor")
        pc, tok = ring.load(sw1_h, 4, shape=(8, 512))
        ring.use(tok)
        p.wait("tensor", se)
        for mt in range(4):
            for kc in range(NCH):
                last = (mt == 3 and kc == NCH - 1)
                p.op("tensor", "matmul", hidp[0:128, mt:mt + 1],
                     pc[:, kc, mt * 128:mt * 128 + 128], xmb_t[:, kc:kc + 1],
                     start=(kc == 0 and mt == 0), stop=last,
                     inc=(te if last else None))
        ring.free(tok, te)
        p.wait("vector", te)
        p.op("vector", "tensor_add", phid_t[:], hidp[0:128, 0:4], sb1_t[:], inc=ve)
        banks.done(5, ve)
        p.wait("scalar", ve)
        p.op("scalar", "activation", phidb_t[:], phid_t[:], AF.Gelu, inc=se)
        banks.use(4, "tensor")
        glp = pbk[4]
        p.wait("tensor", se)
        pc2, tok2 = ring.load(pw2_h, 1, shape=(12, 16))
        ring.use(tok2)
        for pp in range(3):
            for mt in range(4):
                last = (pp == 2 and mt == 3)
                p.op("tensor", "matmul", glp[0:16, pp:pp + 1],
                     pc2[:, pp * 4 + mt, :],
                     hidb_t[:, pp * 4 + mt:pp * 4 + mt + 1],
                     start=(pp == 0 and mt == 0), stop=last,
                     inc=(te if last else None))
        ring.free(tok2, te)
        banks.use(5, "tensor")
        plp = pbk[5]
        ps2, tok3 = ring.load(sw2_h, 1, shape=(4, 3))
        pretoks = [ring.load(pre_h[i], 1, shape=(8, 128)) for i in range(npre)]
        ring.use(tok3)
        for mt in range(4):
            p.op("tensor", "matmul", plp[0:3, 0:1], ps2[:, mt, :],
                 phidb_t[:, mt:mt + 1], start=(mt == 0), stop=(mt == 3),
                 inc=(te if mt == 3 else None))
        ring.free(tok3, te)
        p.wait("vector", te)
        p.op("vector", "tensor_add", eg_t[:], glp[0:16, 0:3], pb2_t[:], inc=ve)
        banks.done(4, ve)
        p.op("vector", "tensor_add", pl3_t[:], plp[0:3, 0:1], sb2_t[:], inc=ve)
        banks.done(5, ve)
        p.wait("scalar", ve)
        for pp in range(3):
            p.wait("scalar", ch)
            p.dma("scalar", g48_t[0:1, pp * 16:pp * 16 + 16],
                  eg_t[0:16, pp:pp + 1], ch)
        p.wait("scalar", ch)
        p.dma("scalar", tiny_t[0:1, C_POL:C_POL + 3], pl3_t[0:3, 0:1], ch)
        p.wait("vector", ch)
        g3x16 = g48_t[0:1, :].rearrange("p (a b) -> p a b", b=16)
        p.op("vector", "reduce_max", tiny_t[0:1, C_SMX:C_SMX + 3], g3x16, AX.X, inc=ve)
        p.wait("gpsimd", ve)
        for pp in range(3):
            p.op("gpsimd", "tensor_scalar", g48_t[0:1, pp * 16:pp * 16 + 16],
                 g48_t[0:1, pp * 16:pp * 16 + 16],
                 tiny_t[0:1, C_SMX + pp:C_SMX + pp + 1], None, ALU.subtract,
                 inc=ge)
        p.wait("scalar", ve)
        p.wait("scalar", ge)
        p.op("scalar", "activation", g48_t[0:1, :], g48_t[0:1, :], AF.Exp, inc=se)
        p.wait("vector", se)
        p.op("vector", "reduce_sum", tiny_t[0:1, C_GS0:C_GS0 + 3], g3x16, AX.X, inc=ve)
        p.op("vector", "reduce_max", tiny_t[0:1, C_PMX:C_PMX + 1],
             tiny_t[0:1, C_POL:C_POL + 3], AX.X, inc=ve)
        p.wait("gpsimd", ve)
        p.op("gpsimd", "tensor_scalar", tiny_t[0:1, C_POL:C_POL + 3],
             tiny_t[0:1, C_POL:C_POL + 3], tiny_t[0:1, C_PMX:C_PMX + 1],
             None, ALU.subtract, inc=ge)
        p.wait("scalar", ve)
        p.wait("scalar", ge)
        p.op("scalar", "activation", tiny_t[0:1, C_POL:C_POL + 3],
             tiny_t[0:1, C_POL:C_POL + 3], AF.Exp, inc=se)
        p.wait("vector", se)
        p.op("vector", "reduce_sum", tiny_t[0:1, C_PS:C_PS + 1],
             tiny_t[0:1, C_POL:C_POL + 3], AX.X, inc=ve)
        p.wait("gpsimd", ve)
        p.op("gpsimd", "tensor_copy", tiny_t[0:1, C_PS2:C_PS2 + 1],
             tiny_t[0:1, C_PS:C_PS + 1], inc=ge)
        p.wait("vector", ge)
        p.op("vector", "reciprocal", tiny_t[0:1, C_PSR:C_PSR + 4],
             tiny_t[0:1, C_PS2:C_PS2 + 4], inc=ve)
        p.op("vector", "reciprocal", tiny_t[0:1, C_GSR:C_GSR + 4],
             tiny_t[0:1, C_GS0:C_GS0 + 4], inc=ve)
        p.wait("gpsimd", ve)
        p.op("gpsimd", "tensor_scalar", tiny_t[0:1, C_POL:C_POL + 3],
             tiny_t[0:1, C_POL:C_POL + 3], tiny_t[0:1, C_PSR:C_PSR + 1],
             None, ALU.mult, inc=ge)
        p.op("gpsimd", "tensor_mul", tiny_t[0:1, C_POL:C_POL + 3],
             tiny_t[0:1, C_POL:C_POL + 3], tiny_t[0:1, C_GSR:C_GSR + 3], inc=ge)
        p.op("gpsimd", "tensor_scalar", hg_t[0:1, :], g48_t[0:1, 0:16],
             tiny_t[0:1, C_POL:C_POL + 1], None, ALU.mult, inc=ge)
        for pp in (1, 2):
            p.op("gpsimd", "tensor_scalar", g48_t[0:1, pp * 16:pp * 16 + 16],
                 g48_t[0:1, pp * 16:pp * 16 + 16],
                 tiny_t[0:1, C_POL + pp:C_POL + pp + 1], None, ALU.mult, inc=ge)
            p.op("gpsimd", "tensor_add", hg_t[0:1, :], hg_t[0:1, :],
                 g48_t[0:1, pp * 16:pp * 16 + 16], inc=ge)
        p.op("gpsimd", "tensor_scalar", srow_t[0:1, 0:16], hg_t[0:1, :],
             1.0 / 8.0, None, ALU.mult, inc=ge)
        p.wait("scalar", ge)
        bcast(gsc_t[:], srow_t[0:1, 0:16])
        p.wait("vector", ch)
        for oc in range(NCH):
            p.op("vector", "tensor_copy", qsc_t[0:64, oc:oc + 1],
                 gsc_t[0:64, 2 * oc:2 * oc + 1])
            p.op("vector", "tensor_copy", qsc_t[64:128, oc:oc + 1],
                 gsc_t[64:128, 2 * oc + 1:2 * oc + 2], inc=ve)
        p.op("vector", "tensor_mul", bqg_t[:], aqb_t[:], qsc_t[:], inc=ve)
        return ve.n, pretoks

    def attn_core_a(gates_ready):
        half = NCH * T // 2
        if T >= 2048:
            sring = [xln_f32[:, f32b:f32b + 512],
                     xln_f32[:, f32b + 512:f32b + 1024],
                     xln_f32[:, f32b + 1024:f32b + 1536],
                     xln_f32[:, half - 1024:half - 512]]
            rscr = xln_f32[:, half - 512:half]
        else:
            sring = [dvs_t[:, 0:512], dvs_t[:, 512:1024]]
            rscr = dvs_t[:, 1024:1536]
        eslot = 0
        pidx = 0
        dq = []

        def emit_S(qt, pr, kc):
            q0 = qt * 512
            w = TTW
            pS = pSA if kc % 2 == 0 else pSB
            b0 = 4 if kc % 2 == 0 else 6
            banks.use(b0, "tensor")
            banks.use(b0 + 1, "tensor")
            ksl = slice(kc * 128, kc * 128 + 128)
            qsl = slice(q0, q0 + w)
            p.op("tensor", "matmul", pS[:, 0:w],
                 B_t[0:64, pr, ksl], A_t[0:64, pr, qsl],
                 start=True, stop=True, tile_position=(0, 0))
            p.op("tensor", "matmul", pS[:, 512:512 + w],
                 B_t[64:128, pr, ksl], A_t[64:128, pr, qsl],
                 start=True, stop=True, tile_position=(64, 0), inc=te)
            off = kc * 128 - q0
            if off >= 0:
                p.wait("vector", te)
                for base in (0, 512):
                    if off > 0:
                        p.op("vector", "tensor_add", pS[:, base:base + off],
                             pS[:, base:base + off], negf_t[:, 0:off])
                    p.op("vector", "tensor_add",
                         pS[:, base + off:base + off + 128],
                         pS[:, base + off:base + off + 128], mask_t[:],
                         inc=(ve if base == 512 else None))
            return off

        def defer_divide(pr, q0, w, ri):
            sv = se.n
            h = w // 2

            def r(base, lo, hi, first=False, sv=sv, ri=ri):
                def f():
                    if first:
                        p.wait("vector", se, sv)
                    p.op("vector", "reciprocal", rscr[base:base + 64, lo:hi],
                         sring[ri][base:base + 64, lo:hi], inc=ve)
                return f

            def m(base, pr=pr, q0=q0, w=w):
                def f():
                    p.op("vector", "tensor_mul",
                         A_t[base:base + 64, pr, q0:q0 + w],
                         A_t[base:base + 64, pr, q0:q0 + w],
                         rscr[base:base + 64, 0:w], inc=ve)
                return f
            dq.extend([r(0, 0, h, True), r(0, h, w), m(0),
                       r(64, 0, h), r(64, h, w), m(64)])

        for qt in range(NT):
            q0 = qt * 512
            w = TTW
            nkc = (q0 + w) // 128
            for pr in range(NCH):
                for b in range(4):
                    banks.use(b, "tensor")
                off = emit_S(qt, pr, 0)
                offs = {0: off}
                e2s = {}
                for kc in range(nkc):
                    pS = pSA if kc % 2 == 0 else pSB
                    b0 = 4 if kc % 2 == 0 else 6
                    e2 = E2buf(eslot)
                    kE = eslot % (NE // 2)
                    eslot += 1
                    pass
                    esl.use(kE, "scalar")
                    p.wait("scalar", te)
                    if offs[kc] >= 0:
                        p.wait("scalar", ve)
                    p.wait("scalar", ch, gates_ready)
                    p.op("scalar", "activation", e2[:, 0:512 + w],
                         pS[:, 0:512 + w], AF.Exp, bias=nba_t[:], inc=se)
                    e2s[kc] = (e2, kE, se.n)
                    banks.done(b0, se)
                    banks.done(b0 + 1, se)
                    if dq:
                        dq.pop(0)()
                    if kc + 1 < nkc:
                        offs[kc + 1] = emit_S(qt, pr, kc + 1)
                    for kv in ([kc - 1] if kc >= 1 else []) + ([kc] if kc == nkc - 1 else []):
                        ev, kEv, sev = e2s.pop(kv)
                        p.wait("tensor", se, sev)
                        st = (kv == 0)
                        sp = (kv == nkc - 1)
                        p.op("tensor", "matmul", pbk[0][0:64, 0:w],
                             C_t[:, kv, pr * 128:pr * 128 + 64], ev[:, 0:w],
                             start=st, stop=sp, tile_position=(0, 0))
                        p.op("tensor", "matmul", pbk[2][0:64, 0:w],
                             ones64_t[:], ev[:, 0:w],
                             start=st, stop=sp, tile_position=(0, 0))
                        p.op("tensor", "matmul", pbk[1][64:128, 0:w],
                             C_t[:, kv, pr * 128 + 64:pr * 128 + 128],
                             ev[:, 512:512 + w],
                             start=st, stop=sp, tile_position=(0, 64))
                        p.op("tensor", "matmul", pbk[3][64:128, 0:w],
                             ones64_t[:], ev[:, 512:512 + w],
                             start=st, stop=sp, tile_position=(0, 64), inc=te)
                        esl.done(kEv, te)
                while len(dq) > 2:
                    dq.pop(0)()
                ri = pidx % len(sring)
                pidx += 1
                p.wait("scalar", te)
                p.op("scalar", "activation", A_t[0:64, pr, q0:q0 + w],
                     pbk[0][0:64, 0:w], AF.Identity, inc=se)
                p.op("scalar", "activation", A_t[64:128, pr, q0:q0 + w],
                     pbk[1][64:128, 0:w], AF.Identity, inc=se)
                banks.done(0, se)
                banks.done(1, se)
                p.wait("vector", te)
                p.op("vector", "tensor_copy", sring[ri][0:64, 0:w],
                     pbk[2][0:64, 0:w], inc=ve)
                p.op("vector", "tensor_copy", sring[ri][64:128, 0:w],
                     pbk[3][64:128, 0:w], inc=ve)
                banks.done(2, ve)
                banks.done(3, ve)
                defer_divide(pr, q0, w, ri)
        while dq:
            dq.pop(0)()

    def wo_resid(seq):
        p.wait("tensor", ve)
        res = []
        for oc in range(NCH):
            pc, tok = ring.load(ao_h[oc], 1, shape=(8, 128))
            ring.use(tok)
            bset = (oc % 2) * 4
            if seq == 0:
                banks.use(bset, "tensor")
                for kc in range(NCH):
                    p.op("tensor", "matmul", pbk[bset][0:128, 0:1],
                         pc[:, kc, :], avbb_t[:, kc:kc + 1],
                         start=(kc == 0), stop=(kc == NCH - 1),
                         inc=(te if kc == NCH - 1 else None))
                p.wait("scalar", te)
                p.op("scalar", "activation", bopr_t[:, oc:oc + 1],
                     pbk[bset][0:128, 0:1], AF.Identity,
                     bias=aob_t[:, oc:oc + 1], inc=se)
                banks.done(bset, se)
                p.wait("vector", se)
            for b in range(bset, bset + NT):
                banks.use(b, "tensor")
            for kc in range(NCH):
                for tt in range(NT):
                    bank = bset + tt
                    p.op("tensor", "matmul", pbk[bank][:, 0:TTW],
                         pc[:, kc, :], A_t[:, kc, tt * 512:tt * 512 + TTW],
                         start=(kc == 0), stop=(kc == NCH - 1),
                         inc=(te if (kc == NCH - 1 and tt == NT - 1) else None))
            ring.free(tok, te)
            p.wait("vector", te)
            for tt in range(NT):
                bank = bset + tt
                p.op("vector", "scalar_tensor_tensor",
                     xres_t[:, oc, tt * 512:tt * 512 + TTW],
                     pbk[bank][:, 0:TTW], bopr_t[:, oc:oc + 1],
                     xres_t[:, oc, tt * 512:tt * 512 + TTW],
                     op0=ALU.add, op1=ALU.add, inc=ve)
                banks.done(bank, ve)
            res.append((ve, ve.n))
        return res


    def ffn(aff, stop_after_h=False, hook=None):
        nh_tiles = NFH * NT
        assert nh_tiles * 512 <= 3 * NCH * T

        def hsl(hc, tt):
            i = hc * NT + tt
            off = i * 512
            if off < NCH * T:
                return A_f[:, off:off + TTW]
            off -= NCH * T
            if off < NCH * T:
                return B_f[:, off:off + TTW]
            off -= NCH * T
            return C_f[:, off:off + TTW]
        h_ve = []
        for hc in range(NFH):
            gu, gtok = ring.load(fgu_h[hc], 2, shape=(16, 128))
            gp = gu[:, 0:8, :]
            up = gu[:, 8:16, :]
            utok = gtok
            ring.use(gtok)
            for tt in range(NT):
                gb = tt % 2
                ub = 2 + tt % 2
                banks.use(gb, "tensor")
                banks.use(ub, "tensor")
                for kc in range(NCH):
                    if hc == 0 and tt == 0:
                        p.wait("tensor", ve, aff[kc])
                    p.op("tensor", "matmul", pbk[gb][:, 0:TTW], gp[:, kc, :],
                         xln_t[:, kc, tt * 512:tt * 512 + TTW],
                         start=(kc == 0), stop=(kc == NCH - 1),
                         inc=(te if kc == NCH - 1 else None))
                for kc in range(NCH):
                    p.op("tensor", "matmul", pbk[ub][:, 0:TTW], up[:, kc, :],
                         xln_t[:, kc, tt * 512:tt * 512 + TTW],
                         start=(kc == 0), stop=(kc == NCH - 1),
                         inc=(te if kc == NCH - 1 else None))
                p.wait("scalar", te)
                p.op("scalar", "activation", hsl(hc, tt), pbk[gb][:, 0:TTW],
                     AF.Silu, scale=INV127, inc=se)
                banks.done(gb, se)
                p.wait("vector", se)
                p.op("vector", "tensor_mul", hsl(hc, tt), pbk[ub][:, 0:TTW],
                     hsl(hc, tt), inc=ve)
                banks.done(ub, ve)
            ring.free(gtok, te)
            h_ve.append(ve.n)
        if stop_after_h:
            return
        for oc in range(NCH):
            dtoks = []
            for j in range(3):
                dpj, dtok = ring.load(fd_h[oc][:, 8 * j:8 * j + 8, :], 2,
                                      shape=(8, 128))
                dtoks.append((dpj, dtok))
            for _, dtok in dtoks:
                ring.use(dtok)
            for tt in range(NT):
                bank = 4 + tt
                banks.use(bank, "tensor")
                for hc in range(NFH):
                    if oc == 0 and tt == 0:
                        p.wait("tensor", ve, h_ve[hc])
                    dpj = dtoks[hc // 8][0]
                    p.op("tensor", "matmul", pbk[bank][:, 0:TTW],
                         dpj[:, hc % 8, :], hsl(hc, tt),
                         start=(hc == 0), stop=(hc == NFH - 1),
                         inc=(te if hc == NFH - 1 else None))
                p.wait("vector", te)
                p.op("vector", "scalar_tensor_tensor",
                     xres_t[:, oc, tt * 512:tt * 512 + TTW],
                     pbk[bank][:, 0:TTW], INV127 * INV127,
                     xres_t[:, oc, tt * 512:tt * 512 + TTW],
                     op0=ALU.mult, op1=ALU.add, inc=ve)
                banks.done(bank, ve)
            if hook is not None:
                hook(oc)
            for _, dtok in dtoks:
                ring.free(dtok, te)


    def attn_b(aff, hook=None):
        proj_fm(bq_h, bqb_t, A_t, aff, chase=True)
        proj_fm(bk_h, bkb_t, B_t, aff)
        ring.barrier()
        ring.n = 4
        bvb = ring_t[:, 4 * 1024:6 * 1024].bitcast(F32)

        p.dma("scalar", bvb[0:1, :], bvrow_h, ch)
        p.wait("scalar", ch)
        bcast(bvb, bvb[0:1, :])
        p.wait("vector", ch)
        proj_tm(bv_h, aff, bias_bcast=bvb)
        bounds(A_t, C_QB)
        bounds(B_t, C_KB)
        p.wait("vector", se)
        p.op("vector", "tensor_mul", tiny_t[0:1, C_BB:C_BB + 1],
             tiny_t[0:1, C_QB:C_QB + 1], tiny_t[0:1, C_KB:C_KB + 1], inc=ve)
        p.wait("gpsimd", ve)
        p.op("gpsimd", "tensor_scalar", tiny_t[0:1, C_BB:C_BB + 1],
             tiny_t[0:1, C_BB:C_BB + 1], -1.0 / 32.0, None, ALU.mult, inc=ge)
        p.wait("scalar", ve)
        p.wait("scalar", ge)

        bcast(nbb_t[:], tiny_t[0:1, C_BB:C_BB + 1])
        nb_ready = ch.n
        for qt in range(NT):
            q0 = qt * 512
            w = TTW
            nkc = (q0 + w) // 128
            ebufs = []
            for kc in range(nkc):
                sbank = 4 + (kc % 2)
                banks.use(sbank, "tensor")
                ksl = slice(kc * 128, kc * 128 + 128)
                for dc in range(NCH):
                    p.op("tensor", "matmul", pbk[sbank][:, 0:w],
                         B_t[:, dc, ksl], A_t[:, dc, q0:q0 + w],
                         start=(dc == 0), stop=(dc == NCH - 1),
                         inc=(te if dc == NCH - 1 else None))
                off = kc * 128 - q0
                if off >= 0:
                    p.wait("vector", te)
                    p.op("vector", "tensor_add", pbk[sbank][:, off:off + 128],
                         pbk[sbank][:, off:off + 128], mask_t[:], inc=ve)
                eb = Ebuf(kc)
                ebufs.append(eb)
                esl.use(kc % NE, "scalar")
                p.wait("scalar", te)
                if off >= 0:
                    p.wait("scalar", ve)
                p.wait("scalar", ch, nb_ready)
                ecols = slice(max(0, off), w)
                p.op("scalar", "activation", eb[:, ecols], pbk[sbank][:, ecols],
                     AF.Exp, scale=1.0 / 32.0, bias=nbb_t[:], inc=se)
                banks.done(sbank, se)
                if off > 0:
                    esl.use(kc % NE, "vector")
                    p.op("vector", "memset", eb[:, 0:off], 0.0, inc=ve)
            e_se = se.n
            e_ve = ve.n
            p.wait("tensor", se, e_se)
            p.wait("tensor", ve, e_ve)
            banks.use(2, "tensor")
            for kc in range(nkc):
                p.op("tensor", "matmul", pbk[2][0:64, 0:w], ones64_t[:],
                     ebufs[kc][:, 0:w], start=(kc == 0), stop=(kc == nkc - 1),
                     inc=(te if kc == nkc - 1 else None))
            p.wait("scalar", te)
            p.op("scalar", "activation", zrow[0:1, q0:q0 + w],
                 pbk[2][0:1, 0:w], AF.Identity, inc=se)
            banks.done(2, se)
            for dc in range(NCH):
                bank = dc % 2
                banks.use(bank, "tensor")
                for kc in range(nkc):
                    p.op("tensor", "matmul", pbk[bank][:, 0:w],
                         C_t[:, kc, dc * 128:dc * 128 + 128],
                         ebufs[kc][:, 0:w],
                         start=(kc == 0), stop=(kc == nkc - 1),
                         inc=(te if kc == nkc - 1 else None))
                if dc == NCH - 1:
                    for kc in range(nkc):
                        esl.done(kc % NE, te)
                p.wait("scalar", te)
                p.op("scalar", "activation", A_t[:, dc, q0:q0 + w],
                     pbk[bank][:, 0:w], AF.Identity, inc=se)
                banks.done(bank, se)
        p.wait("vector", se)
        p.op("vector", "reduce_sum", tiny_t[0:1, C_Z:C_Z + 1],
             zrow[0:1, 0:T], AX.X, inc=ve)
        p.wait("gpsimd", ve)
        p.op("gpsimd", "tensor_copy", tiny_t[0:1, C_PS2:C_PS2 + 1],
             tiny_t[0:1, C_Z:C_Z + 1], inc=ge)
        p.wait("vector", ge)
        p.op("vector", "reciprocal", tiny_t[0:1, C_ZR:C_ZR + 4],
             tiny_t[0:1, C_PS2:C_PS2 + 4], inc=ve)
        p.wait("scalar", ve)

        bcast(rzb_t[:], tiny_t[0:1, C_ZR:C_ZR + 1])
        p.wait("vector", ch)
        p.wait("vector", se)
        for dc in range(NCH):
            for tt in range(NT):
                p.op("vector", "scalar_tensor_tensor",
                     xres_t[:, dc, tt * 512:tt * 512 + TTW],
                     A_t[:, dc, tt * 512:tt * 512 + TTW], rzb_t[:],
                     xres_t[:, dc, tt * 512:tt * 512 + TTW],
                     op0=ALU.mult, op1=ALU.add, inc=ve)
            if hook is not None:
                hook(dc)


    stores_before = 0
    for seq in range(n_dram_seqs):
        p.wait("sync", ve)
        p.wait("sync", qo, stores_before)
        res1 = []
        for c in range(NCH):
            s = qLs[c % 4]
            p.dma("sync", xres_t[:, c, :], x_h[seq, c], s)
            res1.append((s, s.n))

        p.wait("scalar", te)
        p.wait("scalar", ve)
        st = ln_begin()
        for c in range(NCH):
            ln_chunk(st, c, res1[c])
        xr = ln_finish(st, first=True)
        if dbg_stop == "ln1":
            break
        proj_fm(ak_h, akb_t, B_t, xr, chase=True)
        gq, pre_aq = gating(xr, pre_h=aq_h, npre=2)
        if dbg_stop == "gating":
            break
        p.wait("scalar", ve, gq)
        proj_fm(aq_h, bqg_t, A_t, xr, scale_t=qsc_t, pre=pre_aq)
        proj_tm(av_h, xr)
        if dbg_stop == "proj":
            break
        bounds(A_t, C_QA)
        bounds(B_t, C_KA)
        if dbg_stop == "bounds":
            break
        p.wait("vector", se)
        p.op("vector", "tensor_mul", tiny_t[0:1, C_BA:C_BA + 1],
             tiny_t[0:1, C_QA:C_QA + 1], tiny_t[0:1, C_KA:C_KA + 1], inc=ve)
        p.wait("gpsimd", ve)
        p.op("gpsimd", "tensor_scalar", tiny_t[0:1, C_NB8:C_NB8 + 1],
             tiny_t[0:1, C_BA:C_BA + 1], -1.0, None, ALU.mult, inc=ge)
        p.wait("scalar", ge)
        bcast(nba_t[:], tiny_t[0:1, C_NB8:C_NB8 + 1])
        gr = ch.n
        attn_core_a(gr)
        if dbg_stop == "attn_core":
            break
        res = wo_resid(seq)
        ring.barrier()
        if dbg_stop == "attn_a":
            break

        st = ln_begin(scr=lambda c: B_t[:, 4 + c % 4, :])
        for c in range(NCH):
            ln_chunk(st, c, res[c])
        xr = ln_finish(st)
        if dbg_stop == "ln2":
            break
        ffn(xr, stop_after_h=(dbg_stop == "ffn_h"))
        if dbg_stop == "ffn_h":
            break
        ring.barrier()
        if dbg_stop == "ffn1":
            break
        p.wait("scalar", te)
        p.wait("scalar", ve)
        st = ln_begin()
        for c in range(NCH):
            ln_chunk(st, c)
        xr = ln_finish(st)

        st = ln_begin(scr=lambda c: A_t[:, c, :], guard=False)
        attn_b(xr, hook=(lambda dc, s=st: ln_chunk(s, dc, (ve, ve.n))))
        ring.freed[4].append((ve, ve.n))
        ring.freed[5].append((ve, ve.n))
        ring.n = ring.nmax
        ring.barrier()
        if dbg_stop == "attn_b":
            break
        xr = ln_finish(st)

        ffn(xr)
        ring.barrier()

        p.wait("sync", ve)
        for c in range(NCH):
            p.dma("sync", out_h[seq, c], xres_t[:, c, :], qo)
        stores_before = qo.n

    if dbg_stop:
        p.wait("sync", ve)
        p.wait("sync", se)
        p.wait("sync", te)
        p.wait("sync", ch)
        p.dma("sync", dbgA_h, A_t[:], qo)
        p.dma("sync", dbgB_h, B_t[:], qo)
        p.dma("sync", dbgC_h, C_t[:], qo)
        p.dma("sync", dbgX_h, xln_t[:], qo)
        p.dma("sync", dbgR_h, xres_t[:], qo)
        p.dma("sync", dbgS_h[:, 0:16], gsc_t[:], qo)
        p.dma("sync", dbgS_h[:, 16:32], gsc_t[:], qo)
        p.dma("sync", dbgS_h[0:4, 32:96], tiny_t[:], qo)
        p.dma("sync", dbgS_h[0:1, 96:112], hg_t[:], qo)
        p.dma("sync", dbgS_h[0:1, 112:160], g48_t[:], qo)
        stores_before = qo.n
    p.wait("gpsimd", qo, stores_before)
    p.op("gpsimd", "engine_nop")

    with nc.Block() as block:
        p.build(block)
    return nc



def _prep_weights(inputs):
    f = np.float32
    g = {}

    def piece_oc(wT):
        return np.ascontiguousarray(
            wT.reshape(8, 128, 8, 128).transpose(2, 1, 0, 3))

    def piece_col(wT):
        return np.ascontiguousarray(
            wT.reshape(8, 128, 2, 512).transpose(2, 1, 0, 3))

    def pm(b):
        return np.ascontiguousarray(np.asarray(b, f).reshape(8, 128).T)

    wq = np.asarray(inputs["L0_attn_a_wq"], f)
    wk = np.asarray(inputs["L0_attn_a_wk"], f)
    wv = np.asarray(inputs["L0_attn_a_wv"], f)
    wo = np.asarray(inputs["L0_attn_a_wo"], f)
    g["aq"] = piece_oc(np.ascontiguousarray(wq.T))
    g["ak"] = piece_oc(np.ascontiguousarray(wk.T))
    g["ao"] = piece_oc(np.ascontiguousarray(wo.T))
    g["av"] = piece_col(np.ascontiguousarray(wv.T))
    g["aqb"] = pm(inputs["L0_attn_a_bq"])
    g["akb"] = pm(inputs["L0_attn_a_bk"])
    g["avb"] = pm(inputs["L0_attn_a_bv"])
    g["aob"] = pm(inputs["L0_attn_a_bo"])
    pw1 = np.asarray(inputs["L0_attn_a_pw1"], f)
    g["pw1t"] = np.ascontiguousarray(
        pw1.transpose(0, 2, 1).reshape(3, 8, 128, 512).transpose(0, 2, 1, 3))
    pb1 = np.asarray(inputs["L0_attn_a_pb1"], f)
    g["pb1"] = np.ascontiguousarray(
        pb1.reshape(3, 4, 128).transpose(2, 0, 1).reshape(128, 12))
    pw2 = np.asarray(inputs["L0_attn_a_pw2"], f)
    g["pw2t"] = np.ascontiguousarray(
        pw2.transpose(0, 2, 1).reshape(3, 4, 128, 16)
        .transpose(2, 0, 1, 3).reshape(128, 192))
    g["pb2t"] = np.ascontiguousarray(np.asarray(inputs["L0_attn_a_pb2"], f).T)
    sw1 = np.asarray(inputs["L0_attn_a_sw1"], f)
    g["sw1t"] = np.ascontiguousarray(
        sw1.T.reshape(8, 128, 512).transpose(1, 0, 2))
    g["sb1"] = np.ascontiguousarray(
        np.asarray(inputs["L0_attn_a_sb1"], f).reshape(4, 128).T)
    sw2 = np.asarray(inputs["L0_attn_a_sw2"], f)
    g["sw2t"] = np.ascontiguousarray(
        sw2.T.reshape(4, 128, 3).transpose(1, 0, 2).reshape(128, 12))
    g["sb2"] = np.ascontiguousarray(
        np.asarray(inputs["L0_attn_a_sb2"], f).reshape(3, 1))
    Wq = np.asarray(inputs["L0_attn_b_Wq"], f)
    Wk = np.asarray(inputs["L0_attn_b_Wk"], f)
    Wv = np.asarray(inputs["L0_attn_b_Wv"], f)
    g["bq"] = piece_oc(Wq)
    g["bk"] = piece_oc(Wk)
    g["bv"] = piece_col(Wv)
    g["bqb"] = pm(inputs["L0_attn_b_bq"])
    g["bkb"] = pm(inputs["L0_attn_b_bk"])
    g["bvrow"] = np.ascontiguousarray(
        np.asarray(inputs["L0_attn_b_bv"], f).reshape(1, 1024))
    wg = np.asarray(inputs["L0_ffn_w_gate"], np.int32)
    wu = np.asarray(inputs["L0_ffn_w_up"], np.int32)
    wd = np.asarray(inputs["L0_ffn_w_down"], np.int32)
    fg = wg.T.reshape(8, 128, 24, 128).transpose(2, 1, 0, 3)
    fu = wu.T.reshape(8, 128, 24, 128).transpose(2, 1, 0, 3)
    g["fgu"] = np.ascontiguousarray(np.concatenate([fg, fu], axis=2))
    g["fd"] = np.ascontiguousarray(
        wd.T.reshape(24, 128, 8, 128).transpose(2, 1, 0, 3))
    g["mask"] = np.where(np.arange(128)[:, None] > np.arange(128)[None, :],
                         -1e30, 0.0).astype(BF)
    cs = np.zeros((128, 8, 8), BF)
    for j in range(8):
        cs[:, j, j] = 1.0
    g["colsel"] = cs
    return g


def kernel(**inputs):
    T = 2048
    n_cores = 8
    spc = 2
    x = np.asarray(inputs["L0_attn_a_x"], np.float32)
    g = _prep_weights(inputs)
    nc = build_program(T=T, n_dram_seqs=spc)
    in_maps = []
    for c in range(n_cores):
        m = dict(g)
        xs = np.empty((spc, NCH, 128, T), np.float32)
        for s in range(spc):
            xs[s] = x[c * spc + s].T.reshape(NCH, 128, T)
        m["x"] = xs
        in_maps.append(m)
    res = run_bass_kernel_spmd(nc, in_maps, core_ids=list(range(n_cores)))
    out = np.empty_like(x)
    for c in range(n_cores):
        o = np.asarray(res.results[c]["out"])
        for s in range(spc):
            out[c * spc + s] = o[s].reshape(D, T).T
    return out
